# version 03
# baseline (speedup 1.0000x reference)
import contextlib
import numpy as np
import ml_dtypes
import concourse.bass as bass
import concourse.mybir as mybir
from concourse.bass_utils import run_bass_kernel_spmd

F32 = mybir.dt.float32
BF16 = mybir.dt.bfloat16
AF = mybir.ActivationFunctionType
ALU = mybir.AluOpType
AXX = mybir.AxisListType.X
bf16 = ml_dtypes.bfloat16

D = 2048
S = 2048
NSEQ = 2
NCORES = 8
TP = 512
NPASS = S // TP
DC = 16
NPAIR = 64
HC = 64
LN_EPS = 1e-5
RMS_EPS = float(np.finfo(np.float32).eps)
QTHR = float(np.float32(0.5 + 2.0 ** -9))

HEAD_A = [0, 1, 2, 3, 8, 9, 10, 11, 16, 17, 18, 19, 24, 25, 26, 27]
HEAD_PERM = []
for _m in range(16):
    HEAD_PERM += [HEAD_A[_m], HEAD_A[_m] + 4]

ENGINES = ("T", "V", "S", "G", "Y")
PHASES = ("quant", "p1", "kv", "attn", "p3")
DEBUG_SCRATCH = False
BISECT_PROJ_BARRIER = False
BISECT_GU_BARRIER = False
BISECT_H_DEDICATED = False
BISECT_LN_BARRIER = True
BISECT_LNA_BARRIER = True


class Prog:

    def __init__(self, sems):
        self.sems = sems
        self.steps = {e: [] for e in ENGINES}
        self.cnt = {k: 0 for k in sems}
        self.lastw = {}

    ESEMS = ("T", "V", "S", "G")

    def _mkwaits(self, eng, waits):
        closures = []
        for sem, val in waits:
            if val is None or val <= 0:
                continue
            v = val if sem in self.ESEMS else val * 16
            k = (eng, sem)
            if self.lastw.get(k, 0) >= v:
                continue
            self.lastw[k] = v
            h = self.sems[sem]
            closures.append(lambda e, h=h, v=v: e.wait_ge(h, v))
        return closures

    def step(self, eng, fn, waits=(), inc=True, drain=None):
        if drain is None:
            drain = eng in ("V", "G")
        closures = self._mkwaits(eng, waits)
        cnt = None
        h = None
        if inc:
            self.cnt[eng] += 1
            cnt = self.cnt[eng]
            h = self.sems[eng]

        def run(e, fn=fn, closures=closures, h=h, drain=drain):
            for c in closures:
                c(e)
            inst = fn(e)
            if h is not None:
                inst.then_inc(h, 1)
            if drain:
                e.drain()

        self.steps[eng].append(run)
        return cnt

    def dma(self, q, fn, waits=(), stream=None):
        eng = "Y" if q == "DQ" else "G"
        if stream is None:
            stream = q
        closures = self._mkwaits(eng, waits)
        self.cnt[stream] += 1
        h = self.sems[stream]

        def run(e, fn=fn, closures=closures, h=h):
            for c in closures:
                c(e)
            fn(e).then_inc(h, 16)

        self.steps[eng].append(run)
        return self.cnt[stream]

    def barrier(self):
        snap = dict(self.cnt)
        for eng in ENGINES:
            waits = [(s, snap[s]) for s in snap if s != eng]
            closures = self._mkwaits(eng, waits)
            if closures:
                self.steps[eng].append(lambda e, cs=closures: [c(e) for c in cs])


def build_nc(phases=PHASES, debug=False):
    nc = bass.Bass("TRN2", target_bir_lowering=False)

    xT = nc.dram_tensor("xT", [NSEQ, D, S], F32, kind="ExternalInput").ap()
    wgu_h = nc.dram_tensor("wgu_h", [NPAIR, 128, 4096], F32, kind="ExternalInput").ap()
    wp_h = nc.dram_tensor("wp_h", [DC, 128, 8192], F32, kind="ExternalInput").ap()
    wq_t = nc.dram_tensor("wq_t", [DC, 128, 2048], F32, kind="ExternalInput").ap()
    wkv_t = nc.dram_tensor("wkv_t", [DC, 128, 1024], F32, kind="ExternalInput").ap()
    wout_t = nc.dram_tensor("wout_t", [DC, 128, 2048], F32, kind="ExternalInput").ap()
    bq_t = nc.dram_tensor("bq_t", [128, 16], F32, kind="ExternalInput").ap()
    bkv_t = nc.dram_tensor("bkv_t", [128, 4], F32, kind="ExternalInput").ap()
    bv_r = nc.dram_tensor("bv_r", [1, 512], F32, kind="ExternalInput").ap()
    bout_t = nc.dram_tensor("bout_t", [128, 16], F32, kind="ExternalInput").ap()
    cos_h = nc.dram_tensor("cos_h", [128, S], F32, kind="ExternalInput").ap()
    sin_h = nc.dram_tensor("sin_h", [128, S], F32, kind="ExternalInput").ap()
    mask2_h = nc.dram_tensor("mask2_h", [128, 2], BF16, kind="ExternalInput").ap()
    mask2t_h = nc.dram_tensor("mask2t_h", [2, 128], BF16, kind="ExternalInput").ap()
    onesb_h = nc.dram_tensor("onesb_h", [128, 1], BF16, kind="ExternalInput").ap()
    ones1_h = nc.dram_tensor("ones1_h", [1, 128], F32, kind="ExternalInput").ap()
    ones1n_h = nc.dram_tensor("ones1n_h", [1, 128], F32, kind="ExternalInput").ap()
    onesf_h = nc.dram_tensor("onesf_h", [128, 64], F32, kind="ExternalInput").ap()
    onesb1_h = nc.dram_tensor("onesb1_h", [1, 128], BF16, kind="ExternalInput").ap()
    onesbn_h = nc.dram_tensor("onesbn_h", [1, 128], BF16, kind="ExternalInput").ap()
    onesfb_h = nc.dram_tensor("onesfb_h", [128, 64], BF16, kind="ExternalInput").ap()
    r0c_h = nc.dram_tensor("r0c_h", [128, 64], BF16, kind="ExternalInput").ap()

    skind = "ExternalOutput" if debug else "Internal"
    wgu_q = nc.dram_tensor("wgu_q", [NPAIR, 128, 4096], BF16, kind=skind).ap()
    wp_q = nc.dram_tensor("wp_q", [DC, 128, 8192], BF16, kind=skind).ap()
    y1s = nc.dram_tensor("y1s", [NSEQ, D, S], F32, kind=skind).ap()
    y2s = nc.dram_tensor("y2s", [NSEQ, D, S], F32, kind=skind).ap()
    oT = nc.dram_tensor("oT", [NSEQ, D, S], F32, kind="ExternalOutput").ap()
    if debug:
        dxn = nc.dram_tensor("dxn", [16, 128, 8192], BF16, kind="ExternalOutput").ap()
        dh = nc.dram_tensor("dh", [16, 128, 32768], BF16, kind="ExternalOutput").ap()
        drw = nc.dram_tensor("drw", [16, 16, 512], F32, kind="ExternalOutput").ap()
        dps = nc.dram_tensor("dps", [16, 128, 512], F32, kind="ExternalOutput").ap()
        dwp = nc.dram_tensor("dwp", [16, 128, 8192], BF16, kind="ExternalOutput").ap()
        dkv = nc.dram_tensor("dkv", [128, 16512], BF16, kind="ExternalOutput").ap()
        dqa = nc.dram_tensor("dqa", [2, 128, 8192], BF16, kind="ExternalOutput").ap()
        dmr = nc.dram_tensor("dmr", [8, 512], F32, kind="ExternalOutput").ap()
    else:
        dxn = dh = drw = dps = dwp = dkv = dqa = dmr = None

    with contextlib.ExitStack() as es:
        def sb(name, shape, dt):
            return es.enter_context(nc.sbuf_tensor(name, shape, dt)).ap()

        if BISECT_H_DEDICATED:
            AR1 = sb("AR1", [128, 256], F32)
            h_ded = sb("h_ded", [128, 32768], BF16)
        else:
            AR1 = sb("AR1", [128, 16384], F32)
            h_ded = None
        AR2 = sb("AR2", [128, 12288], F32)
        AR3 = sb("AR3", [128, 8192], F32)
        cos_sb = sb("cos_sb", [128, S], F32)
        sin_sb = sb("sin_sb", [128, S], F32)
        xn_sl = sb("xn_sl", [128, 8192], BF16)
        xf = [sb(f"xf{i}", [128, 512], F32) for i in range(2)]
        xb = [sb(f"xb{i}", [128, 512], BF16) for i in range(2)]
        xh = [sb(f"xh{i}", [128, 512], BF16) for i in range(2)]
        sg = [sb(f"sg{i}", [128, 512], F32) for i in range(2)]
        yo = [sb(f"yo{i}", [128, 512], F32) for i in range(2)]
        t2s = sb("t2s", [128, 512], F32)
        rbc = sb("rbc", [128, 512], F32)
        bvbc = sb("bvbc", [128, 512], F32)
        mask2 = sb("mask2", [128, 2], BF16)
        mask2t = sb("mask2t", [2, 128], BF16)
        onesb = sb("onesb", [128, 1], BF16)
        ones1 = sb("ones1", [1, 128], F32)
        ones1n = sb("ones1n", [1, 128], F32)
        onesf = sb("onesf", [128, 64], F32)
        onesb1 = sb("onesb1", [1, 128], BF16)
        onesbn = sb("onesbn", [1, 128], BF16)
        onesfb = sb("onesfb", [128, 64], BF16)
        r0c = sb("r0c", [128, 64], BF16)
        bq8 = sb("bq8", [128, 16], F32)
        bqr = sb("bqr", [128, 16], F32)
        bkv = sb("bkv", [128, 4], F32)
        bout = sb("bout", [128, 16], F32)
        negm = sb("negm", [128, 1], F32)
        ceps = sb("ceps", [128, 2], F32)

        h_sl = h_ded if BISECT_H_DEDICATED else AR1.bitcast(BF16)
        wp_b = [AR2[:, 0:4096].bitcast(BF16), AR2[:, 4096:8192].bitcast(BF16)]
        wg_b = [AR2[:, 8192:10240].bitcast(BF16), AR2[:, 10240:12288].bitcast(BF16)]

        if BISECT_H_DEDICATED:
            ARX = sb("ARX", [128, 32], F32)
            class _Z:
                def __getitem__(self, k): return ARX[0:1, 0:1]
                def bitcast(self, dt): return ARX.bitcast(dt)
                def rearrange(self, *a, **k): return ARX
            ARX_view = _Z()
        else:
            ARX = AR1
        K_sb = ARX[:, 0:4096].bitcast(BF16) if not BISECT_H_DEDICATED else ARX.bitcast(BF16)
        if BISECT_H_DEDICATED:
            V_sb = K_sb; V_v = None; E_rg = [K_sb] * 3
            krow = lambda k: ARX[0:1, 0:1]
            RK = KM = RQ = MS = RL64 = RL32 = r2_sb = mr2_sb = ARX[0:1, 0:1]
            RLH64 = RLL64 = RLH32 = RLL32 = ARX[0:1, 0:1].bitcast(BF16)
            RLBh = RLBl = ARX[0:1, 0:1].bitcast(BF16)
            RL0 = ARX[0:1, 0:1]
        else:
            V_sb = ARX[:, 4096:8256].bitcast(BF16)
            V_v = V_sb.rearrange("p (tc h v) -> p tc h v", tc=16, h=8, v=65)
            E_rg = [ARX[:, 8256 + 256 * i: 8512 + 256 * i].bitcast(BF16) for i in range(3)]
            krow = lambda k: ARX[0:1, 9024 + 512 * k: 9536 + 512 * k]
            RK, KM, RQ, MS = (krow(k) for k in range(4))
            RL64 = ARX[64:65, 9024:9536]
            RL32 = ARX[32:33, 9024:9536]
            RLH64 = ARX[64:65, 9536:10048].bitcast(BF16)[:, 0:512]
            RLL64 = ARX[64:65, 10048:10560].bitcast(BF16)[:, 0:512]
            RLH32 = ARX[32:33, 9536:10048].bitcast(BF16)[:, 0:512]
            RLL32 = ARX[32:33, 10048:10560].bitcast(BF16)[:, 0:512]
            RLBh = AR2[:, 10240:10496].bitcast(BF16)
            RLBl = AR2[:, 10496:10752].bitcast(BF16)
            RL0 = AR2[0:1, 10752:11264]
            r2_sb = ARX[0:1, 12288:14336]
            mr2_sb = ARX[0:1, 14336:16384]
        wkv_b = AR2[:, 0:8192].bitcast(BF16)
        y1_sl = AR2[:, 0:8192]
        wqo_b = [AR2[:, 8192 + 1024 * i: 9216 + 1024 * i].bitcast(BF16) for i in range(2)]
        Q_tl = AR3[:, 0:4096].bitcast(BF16)
        aT_sl = AR3[:, 4096:8192].bitcast(BF16)
        frow = lambda k: AR3[0:1, 512 * k: 512 * (k + 1)]

        q_in = [AR2[:, 0:2048].bitcast(BF16), AR2[:, 2048:4096].bitcast(BF16)]
        q_out = [AR2[:, 4096:6144].bitcast(BF16), AR2[:, 6144:8192].bitcast(BF16)]
        q_th = AR2[:, 8192:12288]
        q_aw = AR3[:, 0:2048].bitcast(BF16)
        q_qv = AR3[:, 2048:4096].bitcast(BF16)
        q_c = AR3[:, 4096:6144].bitcast(BF16)
        q_sbc = AR3[:, 6144:8192].bitcast(BF16)
        q_sc = ([ARX[0:2, 256 * i: 256 * (i + 1)].bitcast(BF16) for i in range(2)]
                if not BISECT_H_DEDICATED else [xb[0][0:2, :], xb[1][0:2, :]])

        PB = [es.enter_context(nc.psum_tensor(f"PB{i}", [128, 512], F32)).ap()
              for i in range(8)]

        sems = {k: es.enter_context(nc.semaphore(k)) for k in
                ("T", "V", "S", "G", "DQ", "GQ",
                 "WG", "WP", "XL", "YS", "WA", "KVL", "QS", "QO")}
        p = Prog(sems)

        def bcast2(bank, row_f32, hi, lo, wV, neg=False, lhs_hi=None, lhs_lo=None,
                   tile_position=None):
            ch = p.step("V", lambda e: e.tensor_copy(hi, row_f32), waits=[("V", wV)])
            cl = p.step("V", lambda e: e.tensor_tensor(lo, row_f32, hi,
                                                       op=ALU.subtract))
            l1 = lhs_hi if lhs_hi is not None else (onesbn if neg else onesb1)[:, :]
            l2 = lhs_lo if lhs_lo is not None else l1
            def mms(e, bank=bank, hi=hi, lo=lo, l1=l1, l2=l2, tp=tile_position):
                kw = {} if tp is None else dict(tile_position=tp)
                e.matmul(bank, l1, hi, start=True, stop=False, **kw)
                return e.matmul(bank, l2, lo, start=False, stop=True, **kw)
            return p.step("T", mms, waits=[("V", cl)])


        for dst, src in [(cos_sb[:], cos_h[:, :]), (sin_sb[:], sin_h[:, :]),
                         (mask2[:], mask2_h[:, :]), (mask2t[:], mask2t_h[:, :]),
                         (onesb[:], onesb_h[:, :]), (ones1[:], ones1_h[:, :]),
                         (ones1n[:], ones1n_h[:, :]), (onesf[:], onesf_h[:, :]),
                         (onesb1[:], onesb1_h[:, :]), (onesbn[:], onesbn_h[:, :]),
                         (onesfb[:], onesfb_h[:, :]), (r0c[:], r0c_h[:, :]),
                         (bqr[:], bq_t[:, :]),
                         (bkv[:], bkv_t[:, :]), (bout[:], bout_t[:, :])]:
            p.dma("DQ", lambda e, dst=dst, src=src: e.dma_start(out=dst, in_=src))
        ld0 = p.cnt["DQ"]
        p.step("S", lambda e: e.activation(bq8[:], bqr[:], AF.Copy, scale=0.125),
               waits=[("DQ", ld0)])
        ldbv = p.dma("DQ", lambda e: e.dma_start(out=yo[0][0:1, :], in_=bv_r[:, :]))
        cwv = p.step("V", lambda e: e.tensor_copy(yo[0][0:1, 0:1], yo[0][0:1, 0:1]),
                     waits=[("DQ", ldbv)])
        cT = bcast2(PB[0][:], yo[0][0:1, :],
                    yo[1][0:1, :].bitcast(BF16)[:, 0:512],
                    t2s[0:1, :].bitcast(BF16)[:, 0:512], cwv)
        p.step("V", lambda e: e.tensor_copy(bvbc[:], PB[0][:]), waits=[("T", cT)])
        p.step("V", lambda e: e.memset(ceps[:, 0:1], LN_EPS))
        p.step("V", lambda e: e.memset(ceps[:, 1:2], RMS_EPS))
        p.barrier()

        qblocks = [(wgu_h[g], wgu_q[g]) for g in range(NPAIR)]
        for d in range(DC):
            qblocks.append((wp_h[d, :, 0:4096], wp_q[d, :, 0:4096]))
            qblocks.append((wp_h[d, :, 4096:8192], wp_q[d, :, 4096:8192]))

        if "quant" in phases:
            lastVo = [0, 0]
            lastGd = [0, 0]
            lastS = [0, 0]
            lastst = [0, 0]
            pV1 = [0, 0]
            pS2 = [0, 0]
            pVa, pGb, pVw, pGd, pVo, pSaw = 0, 0, 0, 0, 0, 0
            for b, (srcb, dstb) in enumerate(qblocks):
                bi = b % 2
                ld = p.dma("GQ", lambda e, srcb=srcb, bi=bi:
                           e.dma_start(out=q_in[bi][:, :], in_=srcb),
                           waits=[("V", lastVo[bi]), ("G", lastGd[bi]),
                                  ("S", lastS[bi])], stream="QS")
                cS_aw = p.step("S", lambda e, bi=bi:
                               e.activation(q_aw[:, :], q_in[bi][:, :], AF.Abs),
                               waits=[("QS", ld), ("V", pVo), ("G", pGd)])
                lastS[bi] = cS_aw
                cT1 = 0
                cS1 = cS2 = 0
                for j in range(8):
                    qi = j % 2
                    sl = slice(512 * j, 512 * (j + 1))
                    cT1 = p.step("T", lambda e, sl=sl, qi=qi:
                                 e.matmul(PB[qi][0:2, :], mask2[:], q_aw[:, sl],
                                          start=True, stop=True),
                                 waits=[("S", cS_aw), ("V", pV1[qi])])
                    cV1 = p.step("V", lambda e, qi=qi:
                                 e.tensor_scalar(q_sc[qi][:, :], PB[qi][0:2, :],
                                                 1.0 / 64, 1e-8,
                                                 op0=ALU.mult, op1=ALU.max),
                                 waits=[("T", cT1)])
                    pV1[qi] = cV1
                    cT2 = p.step("T", lambda e, qi=qi:
                                 e.matmul(PB[2 + qi][:], mask2t[:], q_sc[qi][:, :],
                                          start=True, stop=True),
                                 waits=[("V", cV1), ("S", pS2[qi])])
                    cS1 = p.step("S", lambda e, sl=sl, qi=qi:
                                 e.activation(q_th[:, sl], PB[2 + qi][:], AF.Copy,
                                              scale=QTHR),
                                 waits=[("T", cT2), ("V", pGb)])
                    cS2 = p.step("S", lambda e, sl=sl, qi=qi:
                                 e.activation(q_sbc[:, sl], PB[2 + qi][:], AF.Copy),
                                 waits=[("V", pVw)])
                    pS2[qi] = cS2
                cVa = p.step("V", lambda e, bi=bi:
                             e.tensor_tensor(q_c[:, :], q_in[bi][:, :], q_th[:, :],
                                             op=ALU.is_gt),
                             waits=[("S", cS2), ("G", pGd)])
                cVb = p.step("V", lambda e, bi=bi:
                             e.scalar_tensor_tensor(q_aw[:, :], q_in[bi][:, :], -1.0,
                                                    q_th[:, :], op0=ALU.mult,
                                                    op1=ALU.is_gt),
                             waits=[("T", cT1)])
                cGq = p.step("G", lambda e:
                             e.tensor_tensor(q_qv[:, :], q_c[:, :], q_aw[:, :],
                                             op=ALU.subtract),
                             waits=[("V", cVb)])
                cVw = p.step("V", lambda e:
                             e.tensor_tensor(q_c[:, :], q_qv[:, :], q_sbc[:, :],
                                             op=ALU.mult),
                             waits=[("G", cGq)])
                cGd = p.step("G", lambda e, bi=bi:
                             e.tensor_tensor(q_aw[:, :], q_c[:, :], q_in[bi][:, :],
                                             op=ALU.subtract),
                             waits=[("V", cVw)])
                cVo = p.step("V", lambda e, bi=bi:
                             e.tensor_tensor(q_out[bi][:, :], q_in[bi][:, :],
                                             q_aw[:, :], op=ALU.add),
                             waits=[("G", cGd), ("QO", lastst[bi])])
                lastVo[bi] = cVo
                lastGd[bi] = cGd
                pVa, pGb, pVw, pGd, pVo, pSaw = cVa, cVb, cVw, cGd, cVo, cS_aw
                lastst[bi] = p.dma("DQ", lambda e, dstb=dstb, bi=bi:
                                   e.dma_start(out=dstb, in_=q_out[bi][:, :]),
                                   waits=[("V", cVo)], stream="QO")
            p.barrier()

        def stats_chain(rows, sum_ps, sq_ps, n, eps, eps_ap, r_out, mr_out, wT):
            m, e2, t, v, vep, s0, r0 = (rows(k) for k in range(7))
            p.step("V", lambda e: e.tensor_single_scalar(m, sum_ps, 1.0 / n,
                                                         op=ALU.mult), waits=[("T", wT)])
            p.step("V", lambda e: e.tensor_single_scalar(e2, sq_ps, 1.0 / n,
                                                         op=ALU.mult))
            p.step("V", lambda e: e.tensor_tensor(t, m, m, op=ALU.mult))
            p.step("V", lambda e: e.tensor_tensor(v, e2, t, op=ALU.subtract))
            ce = p.step("V", lambda e: e.tensor_single_scalar(vep, v, eps, op=ALU.add))
            cs = p.step("S", lambda e: e.activation(s0, v, AF.Sqrt, bias=eps_ap),
                        waits=[("V", ce)])
            p.step("V", lambda e: e.reciprocal(r0, s0), waits=[("S", cs)])
            p.step("V", lambda e: e.tensor_tensor(t, r0, r0, op=ALU.mult))
            p.step("V", lambda e: e.tensor_tensor(t, t, vep, op=ALU.mult))
            p.step("V", lambda e: e.tensor_scalar(t, t, -0.5, 1.5, op0=ALU.mult,
                                                  op1=ALU.add))
            cr = p.step("V", lambda e: e.tensor_tensor(r_out, r0, t, op=ALU.mult))
            if mr_out is not None:
                cr = p.step("V", lambda e: e.tensor_tensor(mr_out, m, r_out,
                                                           op=ALU.mult))
            return cr

        def rms_chain(rows, sq_ps, n, wT):
            t, v, vep, s0, r0, r1 = (rows(k) for k in range(6))
            cv = p.step("V", lambda e: e.tensor_single_scalar(v, sq_ps, 1.0 / n,
                                                              op=ALU.mult),
                        waits=[("T", wT)])
            p.step("V", lambda e: e.tensor_single_scalar(vep, v, RMS_EPS, op=ALU.add))
            cs = p.step("S", lambda e: e.activation(s0, v, AF.Sqrt,
                                                    bias=ceps[0:1, 1:2]),
                        waits=[("V", cv)])
            p.step("V", lambda e: e.reciprocal(r0, s0), waits=[("S", cs)])
            p.step("V", lambda e: e.tensor_tensor(t, r0, r0, op=ALU.mult))
            p.step("V", lambda e: e.tensor_tensor(t, t, vep, op=ALU.mult))
            p.step("V", lambda e: e.tensor_scalar(t, t, -0.5, 1.5, op0=ALU.mult,
                                                  op1=ALU.add))
            cr = p.step("V", lambda e: e.tensor_tensor(r1, r0, t, op=ALU.mult))
            return cr, r1

        def ln_stats(src2d, t0):
            uV = [0, 0]
            uS = [0, 0]
            uT = [0, 0]
            cT = None
            for dc in range(DC):
                i = dc % 2
                ld = p.dma("DQ", lambda e, dc=dc, i=i, src2d=src2d, t0=t0:
                           e.dma_start(out=xf[i][:, :],
                                       in_=src2d[128 * dc:128 * (dc + 1), t0:t0 + 512]),
                           waits=[("V", uV[i]), ("S", uS[i])], stream="XL")
                uV[i] = p.step("V", lambda e, i=i:
                               e.tensor_copy(xb[i][:, :], xf[i][:, :]),
                               waits=[("XL", ld), ("T", uT[i])])
                uS[i] = p.step("S", lambda e, i=i:
                               e.activation(xh[i][:, :], xf[i][:, :], AF.Square),
                               waits=[("XL", ld), ("T", uT[i])])
                p.step("T", lambda e, i=i, dc=dc:
                       e.matmul(PB[6][0:1, :], onesb[:], xb[i][:, :],
                                start=(dc == 0), stop=(dc == DC - 1)),
                       waits=[("V", uV[i])], inc=False)
                cT = p.step("T", lambda e, i=i, dc=dc:
                            e.matmul(PB[7][0:1, :], onesb[:], xh[i][:, :],
                                     start=(dc == 0), stop=(dc == DC - 1)),
                            waits=[("S", uS[i])])
                uT[i] = cT
                if BISECT_LN_BARRIER:
                    p.barrier()
            return cT, uV, uS

        def ln_apply(src2d, t0, cr, r_row, mr_row, uV0, uS0):
            bcast2(PB[6][:], r_row, frow(10).bitcast(BF16)[:, 0:512],
                   frow(11).bitcast(BF16)[:, 0:512], cr)
            cb2 = bcast2(PB[7][:], mr_row, frow(12).bitcast(BF16)[:, 0:512],
                         frow(13).bitcast(BF16)[:, 0:512], None, neg=True)
            uV = list(uV0)
            uS = list(uS0)
            lastV = None
            for dc in range(DC):
                i = dc % 2
                ld = p.dma("DQ", lambda e, dc=dc, i=i, src2d=src2d, t0=t0:
                           e.dma_start(out=xf[i][:, :],
                                       in_=src2d[128 * dc:128 * (dc + 1), t0:t0 + 512]),
                           waits=[("V", uV[i]), ("S", uS[i])], stream="XL")
                uS[i] = 0
                p.step("V", lambda e, i=i: e.tensor_tensor(sg[i][:, :], xf[i][:, :],
                                                           PB[6][:], op=ALU.mult),
                       waits=[("XL", ld), ("T", cb2)])
                lastV = p.step("V", lambda e, i=i, dc=dc:
                               e.tensor_tensor(xn_sl[:, 512 * dc:512 * (dc + 1)],
                                               sg[i][:, :], PB[7][:], op=ALU.add))
                uV[i] = lastV
                if BISECT_LN_BARRIER or BISECT_LNA_BARRIER:
                    p.barrier()
            return lastV

        passno = [0]

        def ffn_pass(src2d, dst2d, t0):
            pn = passno[0]
            passno[0] += 1
            p.barrier()
            cT_st, uV0, uS0 = ln_stats(src2d, t0)
            cr = stats_chain(frow, PB[6][0:1, :], PB[7][0:1, :], D, LN_EPS,
                             ceps[0:1, 0:1], frow(8), frow(9), cT_st)
            xn_done = ln_apply(src2d, t0, cr, frow(8), frow(9), uV0, uS0)
            if debug and dxn is not None and pn < 16:
                p.dma("DQ", lambda e, pn=pn: e.dma_start(out=dxn[pn], in_=xn_sl[:, :]),
                      waits=[("V", xn_done)])
                for k in range(10):
                    p.dma("DQ", lambda e, pn=pn, k=k:
                          e.dma_start(out=drw[pn, k:k + 1, :], in_=frow(k)),
                          waits=[("V", cr)])
            p.barrier()
            uS = [0, 0]
            uVh = [0, 0]
            uTn = [0, 0]
            lastTgu = [0, 0]
            ss_last = None
            h_done = None
            for g in range(NPAIR):
                i = g % 2
                pg, pu = PB[2 + 2 * i], PB[3 + 2 * i]
                ld = p.dma("DQ", lambda e, g=g, i=i:
                           e.dma_start(out=wg_b[i][:, :], in_=wgu_q[g]),
                           waits=[("T", lastTgu[i])], stream="WG")

                for sub in range(2):
                    def gu_mms(e, i=i, pg=pg, pu=pu, sub=sub):
                        last = None
                        tgt, off = (pg, 0) if sub == 0 else (pu, 2048)
                        for dc in range(DC):
                            last = e.matmul(tgt[:],
                                            wg_b[i][:, off + 128 * dc:off + 128 * (dc + 1)],
                                            xn_sl[:, 512 * dc:512 * (dc + 1)],
                                            start=(dc == 0), stop=(dc == DC - 1))
                        return last

                    lastTgu[i] = p.step("T", gu_mms,
                                        waits=[("WG", ld), ("V", xn_done),
                                               ("S", uS[i]), ("V", uVh[i])])
                uS[i] = p.step("S", lambda e, i=i, pg=pg:
                               e.activation(sg[i][:, :], pg[:], AF.Silu),
                               waits=[("T", lastTgu[i]), ("V", uVh[i])])
                h_done = uVh[i] = p.step("V", lambda e, i=i, g=g, pu=pu:
                                         e.tensor_tensor(h_sl[:, 512 * g:512 * (g + 1)],
                                                         sg[i][:, :], pu[:],
                                                         op=ALU.mult),
                                         waits=[("S", uS[i])])
                cSq = p.step("S", lambda e, i=i, g=g:
                             e.activation(xh[i][:, :],
                                          h_sl[:, 512 * g:512 * (g + 1)], AF.Square),
                             waits=[("V", uVh[i]), ("T", uTn[i])])
                ss_last = uTn[i] = p.step("T", lambda e, i=i, g=g:
                                          e.matmul(PB[6][0:1, :], onesb[:], xh[i][:, :],
                                                   start=(g == 0), stop=(g == NPAIR - 1)),
                                          waits=[("S", cSq)])
                if BISECT_GU_BARRIER:
                    p.barrier()
            if debug and dh is not None and pn < 16:
                p.dma("DQ", lambda e, pn=pn: e.dma_start(out=dh[pn], in_=h_sl[:, :]),
                      waits=[("V", h_done)])
            crh, rh = rms_chain(frow, PB[6][0:1, :], 8192, ss_last)
            cbr = bcast2(PB[7][:], rh, frow(10).bitcast(BF16)[:, 0:512],
                         frow(11).bitcast(BF16)[:, 0:512], crh)
            cvr = p.step("V", lambda e: e.tensor_copy(rbc[:], PB[7][:]),
                         waits=[("T", cbr)])
            if debug and drw is not None and pn < 16:
                for k in range(10, 16):
                    p.dma("DQ", lambda e, pn=pn, k=k:
                          e.dma_start(out=drw[pn, k:k + 1, :], in_=frow(k - 10)),
                          waits=[("V", crh)])
            lastTp = [0, 0]
            uVu = [0, 0]
            uVy = [0, 0]
            stc = [0, 0]
            for dcp in range(DC):
                i = dcp % 2
                pp = PB[i]
                ldw = p.dma("DQ", lambda e, dcp=dcp, i=i:
                            e.dma_start(out=wp_b[i][:, :], in_=wp_q[dcp]),
                            waits=[("T", lastTp[i])], stream="WP")
                ldx = p.dma("DQ", lambda e, dcp=dcp, i=i, src2d=src2d, t0=t0:
                            e.dma_start(out=xf[i][:, :],
                                        in_=src2d[128 * dcp:128 * (dcp + 1), t0:t0 + 512]),
                            waits=[("V", uVy[i])], stream="XL")

                for sub in range(4):
                    def proj_mms(e, i=i, pp=pp, sub=sub):
                        last = None
                        for hc in range(16 * sub, 16 * (sub + 1)):
                            last = e.matmul(pp[:], wp_b[i][:, 128 * hc:128 * (hc + 1)],
                                            h_sl[:, 512 * hc:512 * (hc + 1)],
                                            start=(hc == 0), stop=(hc == HC - 1))
                        return last

                    lastTp[i] = p.step("T", proj_mms,
                                       waits=[("WP", ldw), ("V", uVu[i]),
                                              ("V", h_done)])
                uVu[i] = p.step("V", lambda e, i=i, pp=pp:
                                e.tensor_tensor(sg[i][:, :], pp[:], rbc[:],
                                                op=ALU.mult),
                                waits=[("T", lastTp[i]), ("V", cvr)])
                uVy[i] = p.step("V", lambda e, i=i:
                                e.scalar_tensor_tensor(yo[i][:, :], sg[i][:, :], 0.5,
                                                       xf[i][:, :], op0=ALU.mult,
                                                       op1=ALU.add),
                                waits=[("XL", ldx), ("YS", stc[i])])
                stc[i] = p.dma("DQ", lambda e, dcp=dcp, i=i, dst2d=dst2d, t0=t0:
                               e.dma_start(out=dst2d[128 * dcp:128 * (dcp + 1),
                                                     t0:t0 + 512],
                                           in_=yo[i][:, :]),
                               waits=[("V", uVy[i])], stream="YS")
                if debug and dps is not None and pn == 0:
                    cpe = p.step("S", lambda e, i=i, pp=pp:
                                 e.activation(t2s[:], pp[:], AF.Copy),
                                 waits=[("V", uVu[i]), ("T", lastTp[i])])
                    p.dma("DQ", lambda e, dcp=dcp: e.dma_start(out=dps[dcp],
                                                               in_=t2s[:, :]),
                          waits=[("S", cpe)])
                    p.dma("DQ", lambda e, dcp=dcp, i=i:
                          e.dma_start(out=dwp[dcp], in_=wp_b[i][:, :]),
                          waits=[("T", lastTp[i])])
                    p.barrier()
                if BISECT_PROJ_BARRIER:
                    p.barrier()

        def rope_chunk(src_f32, dst_bf, t0):
            c1 = p.step("V", lambda e: e.stream_shuffle(t2s[:], src_f32,
                                                        [i ^ 1 for i in range(32)]))
            cg = p.step("G", lambda e, t0=t0:
                        e.tensor_tensor(t2s[:], t2s[:], sin_sb[:, t0:t0 + 512],
                                        op=ALU.mult),
                        waits=[("V", c1)])
            p.step("V", lambda e, t0=t0:
                   e.tensor_tensor(sg[1][:, :], src_f32, cos_sb[:, t0:t0 + 512],
                                   op=ALU.mult))
            cv = p.step("V", lambda e, dst_bf=dst_bf:
                        e.tensor_tensor(dst_bf, sg[1][:, :], t2s[:], op=ALU.add),
                        waits=[("G", cg)])
            return cv

        for s in range(NSEQ):
            xT_s, y1_s, y2_s, oT_s = xT[s], y1s[s], y2s[s], oT[s]

            if "p1" in phases:
                for pi in range(NPASS):
                    ffn_pass(xT_s, y1_s, pi * TP)
                p.barrier()

            if "kv" in phases:
                p.barrier()
                p.step("V", lambda e: e.memset(V_v[:, :, :, 64:65], 1.0))
                p.step("V", lambda e: e.memset(RK, 0.0))
                ldkv = p.dma("GQ", lambda e: e.dma_start(
                    out=wkv_b[:].rearrange("p (c q) -> p c q", q=1024),
                    in_=wkv_t[:, :, :].rearrange("c p q -> p c q")),
                    stream="KVL")
                for blk in range(NPASS):
                    t0 = blk * TP
                    p.barrier()
                    cT_st, uV0, uS0 = ln_stats(y1_s, t0)
                    cr2 = stats_chain(frow, PB[6][0:1, :], PB[7][0:1, :], D, LN_EPS,
                                      ceps[0:1, 0:1], r2_sb[0:1, t0:t0 + 512],
                                      mr2_sb[0:1, t0:t0 + 512], cT_st)
                    xn2_done = ln_apply(y1_s, t0, cr2, r2_sb[0:1, t0:t0 + 512],
                                        mr2_sb[0:1, t0:t0 + 512], uV0, uS0)
                    p.barrier()
                    pnn = 0
                    pVmax = [0]
                    pkT = [(0, 0), (0, 0)]
                    for kc in range(4):
                        i = kc % 2
                        pk = PB[2 + i]

                        def k_mms(e, kc=kc, pk=pk):
                            last = None
                            for dc in range(DC):
                                last = e.matmul(pk[:],
                                                wkv_b[:, 1024 * dc + 128 * kc:
                                                      1024 * dc + 128 * (kc + 1)],
                                                xn_sl[:, 512 * dc:512 * (dc + 1)],
                                                start=(dc == 0), stop=(dc == DC - 1))
                            return last

                        cTk = p.step("T", k_mms,
                                     waits=[("KVL", ldkv), ("V", xn2_done),
                                            ("S", pkT[i][0]), ("V", pkT[i][1])])
                        cVk = p.step("V", lambda e, kc=kc, pk=pk:
                                     e.tensor_single_scalar(sg[0][:, :], pk[:],
                                                            bkv[:, kc:kc + 1],
                                                            op=ALU.add),
                                     waits=[("T", cTk)])
                        cSq = p.step("S", lambda e, kc=kc, pk=pk:
                                     e.activation(xh[0][:, :], pk[:], AF.Square,
                                                  bias=bkv[:, kc:kc + 1]),
                                     waits=[("T", pnn), ("V", cVk)])
                        pkT[i] = (cSq, cVk)
                        pnn = p.step("T", lambda e: e.matmul(PB[6][0:1, :], onesb[:],
                                                             xh[0][:, :],
                                                             start=True, stop=True),
                                     waits=[("S", cSq), ("V", pVmax[0])])
                        pVmax[0] = p.step("V", lambda e, pnn=pnn:
                                          e.tensor_tensor(RK, RK, PB[6][0:1, :],
                                                          op=ALU.max),
                                          waits=[("T", pnn)])
                        rope_chunk(sg[0][:, :],
                                   K_sb[:, 2048 * kc + t0: 2048 * kc + t0 + 512], t0)
                    pvV = [0, 0]
                    for tb in range(4):
                        i = tb % 2
                        pv = PB[4 + i]

                        def v_mms(e, tb=tb, pv=pv):
                            last = None
                            for dc in range(DC):
                                last = e.matmul(pv[:],
                                                xn_sl[:, 512 * dc + 128 * tb:
                                                      512 * dc + 128 * (tb + 1)],
                                                wkv_b[:, 1024 * dc + 512:
                                                      1024 * (dc + 1)],
                                                start=(dc == 0), stop=(dc == DC - 1))
                            return last

                        cTv = p.step("T", v_mms, waits=[("V", pvV[i])])
                        tc = blk * 4 + tb
                        pvV[i] = p.step("V", lambda e, tc=tc, pv=pv:
                                        e.tensor_tensor(
                                            V_v[:, tc, :, 0:64],
                                            pv[:].rearrange("p (h v) -> p h v", v=64),
                                            bvbc[:].rearrange("p (h v) -> p h v", v=64),
                                            op=ALU.add),
                                        waits=[("T", cTv)])
                p.step("V", lambda e: e.reduce_max(KM[:, 0:1], RK, axis=AXX))
                p.barrier()
                if debug and dkv is not None and s == 0:
                    p.dma("DQ", lambda e: e.dma_start(out=dkv[:, 0:8192], in_=K_sb[:, :]))
                    p.dma("DQ", lambda e: e.dma_start(out=dkv[:, 8192:16512],
                                                      in_=V_sb[:, :]))
                    p.dma("DQ", lambda e: e.dma_start(out=dmr[0:1, :], in_=RK))
                    p.dma("DQ", lambda e: e.dma_start(out=dmr[1:2, :], in_=KM))
                    p.dma("DQ", lambda e: e.dma_start(out=dmr[2:3, :],
                                                      in_=r2_sb[0:1, 0:512]))
                    p.dma("DQ", lambda e: e.dma_start(out=dmr[3:4, :],
                                                      in_=mr2_sb[0:1, 0:512]))
                    p.barrier()

            if "attn" in phases:
                p.step("V", lambda e: e.memset(RLBh[:, :], 0.0))
                p.step("V", lambda e: e.memset(RLBl[:, :], 0.0))
                for qt in range(NPASS):
                    q0 = qt * TP
                    p.barrier()
                    y1lds = []
                    for dc in range(DC):
                        y1lds.append(p.dma("DQ", lambda e, dc=dc, y1_s=y1_s, q0=q0:
                                           e.dma_start(
                                               out=y1_sl[:, 512 * dc:512 * (dc + 1)],
                                               in_=y1_s[128 * dc:128 * (dc + 1),
                                                        q0:q0 + 512]),
                                           stream="XL"))
                    cbA = bcast2(PB[6][:], r2_sb[0:1, q0:q0 + 512],
                                 krow(4).bitcast(BF16)[:, 0:512],
                                 krow(5).bitcast(BF16)[:, 0:512], None)
                    p.step("V", lambda e: e.tensor_copy(MS[:, 8:9], MS[:, 8:9]),
                           waits=[("T", cbA)])
                    cb2 = bcast2(PB[7][:], mr2_sb[0:1, q0:q0 + 512],
                                 krow(4).bitcast(BF16)[:, 0:512],
                                 krow(5).bitcast(BF16)[:, 0:512], None, neg=True)
                    xn2_done = None
                    for dc in range(DC):
                        p.step("V", lambda e, dc=dc:
                               e.tensor_tensor(sg[dc % 2][:, :],
                                               y1_sl[:, 512 * dc:512 * (dc + 1)],
                                               PB[6][:], op=ALU.mult),
                               waits=[("XL", y1lds[dc]), ("T", cb2)])
                        xn2_done = p.step("V", lambda e, dc=dc:
                                          e.tensor_tensor(
                                              xn_sl[:, 512 * dc:512 * (dc + 1)],
                                              sg[dc % 2][:, :], PB[7][:], op=ALU.add))
                    p.step("V", lambda e: e.memset(RQ, 0.0))
                    p.barrier()
                    lastTq = [0, 0]
                    uSq2 = [0, 0]
                    uVq = [0, 0]
                    pnn = 0
                    pVmax = [0]
                    for qc in range(16):
                        i = qc % 2
                        pq = PB[i]
                        ldw = p.dma("GQ", lambda e, qc=qc, i=i:
                                    e.dma_start(out=wqo_b[i][:].rearrange(
                                        "p (c q) -> p c q", q=128),
                                        in_=wq_t[:, :, 128 * qc:128 * (qc + 1)].rearrange(
                                            "c p q -> p c q")),
                                    waits=[("T", lastTq[i])], stream="WA")

                        def q_mms(e, i=i, pq=pq):
                            last = None
                            for dc in range(DC):
                                last = e.matmul(pq[:],
                                                wqo_b[i][:, 128 * dc:128 * (dc + 1)],
                                                xn_sl[:, 512 * dc:512 * (dc + 1)],
                                                start=(dc == 0), stop=(dc == DC - 1))
                            return last

                        lastTq[i] = p.step("T", q_mms,
                                           waits=[("WA", ldw), ("V", xn2_done),
                                                  ("S", uSq2[i]), ("V", uVq[i])])
                        p.step("V", lambda e, qc=qc, pq=pq:
                               e.tensor_single_scalar(sg[0][:, :], pq[:],
                                                      bqr[:, qc:qc + 1],
                                                      op=ALU.add),
                               waits=[("T", lastTq[i])])
                        uVq[i] = p.step("V", lambda e:
                                        e.tensor_single_scalar(yo[0][:, :],
                                                               sg[0][:, :], 0.125,
                                                               op=ALU.mult))
                        uSq2[i] = p.step("S", lambda e, qc=qc, pq=pq:
                                         e.activation(xh[0][:, :], pq[:], AF.Square,
                                                      bias=bq8[:, qc:qc + 1],
                                                      scale=0.125),
                                         waits=[("T", pnn), ("V", uVq[i])])
                        pnn = p.step("T", lambda e: e.matmul(PB[6][0:1, :], onesb[:],
                                                             xh[0][:, :],
                                                             start=True, stop=True),
                                     waits=[("S", uSq2[i]), ("V", pVmax[0])])
                        pVmax[0] = p.step("V", lambda e, pnn=pnn:
                                          e.tensor_tensor(RQ, RQ, PB[6][0:1, :],
                                                          op=ALU.max),
                                          waits=[("T", pnn)])
                        rope_chunk(yo[0][:, :], Q_tl[:, 512 * qc:512 * (qc + 1)], q0)
                    p.step("V", lambda e: e.reduce_max(MS[:, 0:1], RQ, axis=AXX))
                    cvm = p.step("V", lambda e: e.tensor_tensor(MS[:, 1:2], MS[:, 0:1],
                                                                KM[:, 0:1], op=ALU.mult))
                    csm = p.step("S", lambda e: e.activation(MS[:, 2:3], MS[:, 1:2],
                                                             AF.Sqrt),
                                 waits=[("V", cvm)])
                    cvn = p.step("V", lambda e: e.tensor_single_scalar(
                        MS[:, 3:4], MS[:, 2:3], -1.0, op=ALU.mult), waits=[("S", csm)])
                    cvn = p.step("V", lambda e: e.tensor_copy(
                        MS.bitcast(BF16)[:, 8:9], MS[:, 3:4]))
                    ctm = p.step("T", lambda e: e.matmul(PB[6][:, 0:1], onesb1[:, :],
                                                         MS.bitcast(BF16)[:, 8:9],
                                                         start=True, stop=True),
                                 waits=[("V", cvn)])
                    cvb = p.step("V", lambda e: e.tensor_copy(negm[:, 0:1],
                                                              PB[6][:, 0:1]),
                                 waits=[("T", ctm)])
                    p.barrier()
                    uExpBank = [0, 0]
                    pcrl1 = [0]
                    uPV = {0: 0, 1: 0, 2: 0}
                    lastVat = [0, 0]
                    pcvrl = 0
                    for hh in range(32):
                        m, slot = hh // 2, hh % 2
                        head = HEAD_PERM[hh]
                        g = head // 4
                        kc = head // 8
                        off = 64 * slot
                        pat = PB[5 + 2 * (hh % 2)]
                        for kb in range(16):
                            pss = PB[3 + kb % 2]
                            cqk = p.step("T",
                                         lambda e, kb=kb, kc=kc, off=off, m=m, pss=pss:
                                         e.matmul(pss[:],
                                                  K_sb[off:off + 64,
                                                       2048 * kc + 128 * kb:
                                                       2048 * kc + 128 * (kb + 1)],
                                                  Q_tl[off:off + 64,
                                                       512 * m:512 * (m + 1)],
                                                  start=True, stop=True,
                                                  tile_position=(off, 0)),
                                         waits=[("V", cvb), ("S", uExpBank[kb % 2])])
                            cexp = p.step("S", lambda e, kb=kb, pss=pss:
                                          e.activation(E_rg[kb % 3][:, :], pss[:],
                                                       AF.Exp, bias=negm[:, 0:1]),
                                          waits=[("T", cqk), ("T", uPV[kb % 3])])
                            uExpBank[kb % 2] = cexp
                            if slot == 0:
                                uPV[kb % 3] = p.step(
                                    "T", lambda e, kb=kb, g=g, pat=pat:
                                    e.matmul(pat[0:65, :], V_v[:, kb, g, :],
                                             E_rg[kb % 3][:, :],
                                             start=(kb == 0), stop=(kb == 15)),
                                    waits=[("S", cexp), ("V", lastVat[hh % 2])])
                            else:
                                p.step("T", lambda e, kb=kb, g=g, pat=pat:
                                       e.matmul(pat[64:128, :],
                                                V_v[:, kb, g, 0:64],
                                                E_rg[kb % 3][:, :],
                                                start=(kb == 0), stop=(kb == 15),
                                                tile_position=(0, 64)),
                                       waits=[("S", cexp), ("V", lastVat[hh % 2])],
                                       inc=False)
                                uPV[kb % 3] = p.step(
                                    "T", lambda e, kb=kb:
                                    e.matmul(PB[2][0:1, :], onesb[:],
                                             E_rg[kb % 3][:, :],
                                             start=(kb == 0), stop=(kb == 15)),
                                    waits=[("V", pcrl1[0])] if kb == 0 else ())
                        if slot == 0:
                            crl = p.step("V", lambda e, pat=pat:
                                         e.reciprocal(RL64, pat[64:65, :]),
                                         waits=[("T", uPV[15 % 3])])
                            ctrl = bcast2(PB[6][0:64, :], RL64, RLH64, RLL64, pcvrl,
                                          lhs_hi=onesfb[64:65, 0:64],
                                          tile_position=(64, 0))
                            pcvrl = p.step("V", lambda e, i=hh % 2:
                                           e.tensor_copy(xf[i][0:64, :],
                                                         PB[6][0:64, :]),
                                           waits=[("T", ctrl)])
                            lastVat[hh % 2] = p.step(
                                "V", lambda e, pat=pat, i=hh % 2, m=m:
                                e.tensor_tensor(aT_sl[0:64, 512 * m:512 * (m + 1)],
                                                pat[0:64, :], xf[i][0:64, :],
                                                op=ALU.mult))
                        else:
                            crl = p.step("V", lambda e:
                                         e.reciprocal(RL0, PB[2][0:1, :]),
                                         waits=[("T", uPV[15 % 3])])
                            pcrl1[0] = crl
                            p.step("V", lambda e: e.tensor_copy(RLBh[0:1, :], RL0),
                                   waits=[("V", pcvrl)])
                            cl2 = p.step("V", lambda e:
                                         e.tensor_tensor(RLBl[0:1, :], RL0,
                                                         RLBh[0:1, :],
                                                         op=ALU.subtract))
                            def rl1_mms(e):
                                e.matmul(PB[6][64:128, :], r0c[:], RLBh[:, :],
                                         start=True, stop=False,
                                         tile_position=(0, 64))
                                return e.matmul(PB[6][64:128, :], r0c[:], RLBl[:, :],
                                                start=False, stop=True,
                                                tile_position=(0, 64))
                            ctrl = p.step("T", rl1_mms, waits=[("V", cl2)])
                            pcvrl = p.step("V", lambda e, i=hh % 2:
                                           e.tensor_copy(xf[i][64:128, :],
                                                         PB[6][64:128, :]),
                                           waits=[("T", ctrl)])
                            lastVat[hh % 2] = p.step(
                                "V", lambda e, pat=pat, i=hh % 2, m=m:
                                e.tensor_tensor(aT_sl[64:128, 512 * m:512 * (m + 1)],
                                                pat[64:128, :], xf[i][64:128, :],
                                                op=ALU.mult))
                    p.barrier()
                    if debug and dqa is not None and s == 0 and qt == 0:
                        p.dma("DQ", lambda e: e.dma_start(out=dqa[0], in_=Q_tl[:, :]))
                        p.dma("DQ", lambda e: e.dma_start(out=dqa[1], in_=aT_sl[:, :]))
                        p.dma("DQ", lambda e: e.dma_start(out=dmr[4:5, :], in_=RQ))
                        p.dma("DQ", lambda e: e.dma_start(out=dmr[5:6, :], in_=MS))
                        p.dma("DQ", lambda e: e.dma_start(out=dmr[6:7, 0:1],
                                                          in_=negm[0:1, 0:1]))
                        p.barrier()
                    lastTo = [0, 0]
                    uVo = [0, 0]
                    for dc in range(DC):
                        i = dc % 2
                        po = PB[i]
                        ldw = p.dma("GQ", lambda e, dc=dc, i=i:
                                    e.dma_start(out=wqo_b[i][:].rearrange(
                                        "p (c q) -> p c q", q=128),
                                        in_=wout_t[:, :, 128 * dc:
                                                   128 * (dc + 1)].rearrange(
                                            "c p q -> p c q")),
                                    waits=[("T", lastTo[i])], stream="WA")

                        def o_mms(e, i=i, po=po):
                            last = None
                            for ac in range(16):
                                last = e.matmul(po[:],
                                                wqo_b[i][:, 128 * ac:128 * (ac + 1)],
                                                aT_sl[:, 512 * ac:512 * (ac + 1)],
                                                start=(ac == 0), stop=(ac == 15))
                            return last

                        lastTo[i] = p.step("T", o_mms,
                                           waits=[("WA", ldw), ("V", uVo[i])])
                        p.step("V", lambda e, i=i, dc=dc, po=po:
                               e.tensor_single_scalar(sg[i][:, :], po[:],
                                                      bout[:, dc:dc + 1],
                                                      op=ALU.add),
                               waits=[("T", lastTo[i])])
                        uVo[i] = p.step("V", lambda e, i=i, dc=dc:
                                        e.tensor_tensor(
                                            yo[i][:, :], sg[i][:, :],
                                            y1_sl[:, 512 * dc:512 * (dc + 1)],
                                            op=ALU.add))
                        p.dma("DQ", lambda e, dc=dc, i=i, q0=q0, y2_s=y2_s:
                              e.dma_start(out=y2_s[128 * dc:128 * (dc + 1),
                                                   q0:q0 + 512],
                                          in_=yo[i][:, :]),
                              waits=[("V", uVo[i])], stream="YS")
                p.barrier()

            if "p3" in phases:
                for pi in range(NPASS):
                    ffn_pass(y2_s, oT_s, pi * TP)
                p.barrier()

        finals = [(sems[k], 16 * p.cnt[k]) for k in p.cnt
                  if k not in ("T", "V", "S", "G") and p.cnt[k] > 0]
        p.steps["Y"].append(lambda e, fs=finals: [e.wait_ge(h, v) for h, v in fs])

        with nc.Block() as blk:
            @blk.tensor
            def _(t):
                for fn in p.steps["T"]:
                    fn(t)

            @blk.vector
            def _(v):
                for fn in p.steps["V"]:
                    fn(v)

            @blk.scalar
            def _(sc):
                for fn in p.steps["S"]:
                    fn(sc)

            @blk.gpsimd
            def _(g):
                for fn in p.steps["G"]:
                    fn(g)

            @blk.sync
            def _(sy):
                for fn in p.steps["Y"]:
                    fn(sy)

    return nc



def _host_inputs(L0_attn_x, L0_attn_W_q, L0_attn_b_q, L0_attn_W_kv, L0_attn_b_kv,
                 L0_attn_W_out, L0_attn_b_out, L0_ffn_w_gate_up, L0_ffn_w_proj):
    f32 = np.float32
    perm = np.array(HEAD_PERM, dtype=np.int64)

    wgu = np.ascontiguousarray(L0_ffn_w_gate_up, dtype=f32)
    wp = np.ascontiguousarray(L0_ffn_w_proj, dtype=f32)
    wgu_h = np.ascontiguousarray(
        wgu.reshape(2, 64, 128, 16, 128).transpose(1, 4, 0, 3, 2)).reshape(64, 128, 4096)
    wp_h = np.ascontiguousarray(
        wp.reshape(16, 128, 64, 128).transpose(0, 3, 2, 1)).reshape(16, 128, 8192)

    Wq_p = np.ascontiguousarray(
        np.asarray(L0_attn_W_q).reshape(32, 64, D)[perm].reshape(D, D))
    wq_t = np.ascontiguousarray(Wq_p.T).reshape(16, 128, 2048)
    bq_p = np.ascontiguousarray(np.asarray(L0_attn_b_q).reshape(32, 64)[perm]).reshape(D)
    bq_t = np.ascontiguousarray(bq_p.reshape(16, 128).T)

    wkv_t = np.ascontiguousarray(np.asarray(L0_attn_W_kv).T).reshape(16, 128, 1024)
    bkv_t = np.ascontiguousarray(np.asarray(L0_attn_b_kv)[:512].reshape(4, 128).T)
    bv_r = np.ascontiguousarray(np.asarray(L0_attn_b_kv)[512:].reshape(1, 512))

    Wout_p = np.ascontiguousarray(
        np.asarray(L0_attn_W_out).T.reshape(32, 64, D)[perm].reshape(D, D))
    wout_t = Wout_p.reshape(16, 128, 2048)
    bout_t = np.ascontiguousarray(np.asarray(L0_attn_b_out).reshape(16, 128).T)

    pos = np.arange(S, dtype=f32)
    freq = (f32(1.0) / np.float32(10000.0) ** (np.arange(0, 64, 2, dtype=f32) / f32(64)))
    theta = pos[None, :] * freq[:, None]
    cos32 = np.cos(theta).astype(f32)
    sin32 = np.sin(theta).astype(f32)
    cos_t = np.zeros((128, S), f32)
    sin_t = np.zeros((128, S), f32)
    for hs in range(2):
        for i in range(32):
            cos_t[64 * hs + 2 * i] = cos32[i]
            cos_t[64 * hs + 2 * i + 1] = cos32[i]
            sin_t[64 * hs + 2 * i] = -sin32[i]
            sin_t[64 * hs + 2 * i + 1] = sin32[i]

    mask2 = np.zeros((128, 2), bf16)
    mask2[0:64, 0] = 1
    mask2[64:128, 1] = 1
    mask2t = np.ascontiguousarray(mask2.T)
    onesb = np.ones((128, 1), bf16)
    ones1 = np.ones((1, 128), f32)
    ones1n = -np.ones((1, 128), f32)
    onesf = np.ones((128, 64), f32)
    onesb1 = np.ones((1, 128), bf16)
    onesbn = -np.ones((1, 128), bf16)
    onesfb = np.ones((128, 64), bf16)
    r0c = np.zeros((128, 64), bf16); r0c[0, :] = 1

    shared = dict(wgu_h=wgu_h, wp_h=wp_h, wq_t=wq_t, wkv_t=wkv_t, wout_t=wout_t,
                  bq_t=bq_t, bkv_t=bkv_t, bv_r=bv_r, bout_t=bout_t,
                  cos_h=cos_t, sin_h=sin_t, mask2_h=mask2, mask2t_h=mask2t,
                  onesb_h=onesb, ones1_h=ones1, ones1n_h=ones1n, onesf_h=onesf,
                  onesb1_h=onesb1, onesbn_h=onesbn, onesfb_h=onesfb,
                  r0c_h=r0c)
    in_maps = []
    for c in range(NCORES):
        xc = np.ascontiguousarray(
            np.asarray(L0_attn_x)[NSEQ * c:NSEQ * (c + 1)].transpose(0, 2, 1),
            dtype=f32)
        m = dict(shared)
        m["xT"] = xc
        in_maps.append(m)
    return in_maps


def kernel(**inputs):
    in_maps = _host_inputs(**inputs)
    nc = build_nc(phases=PHASES, debug=DEBUG_SCRATCH)
    res = run_bass_kernel_spmd(nc, in_maps, core_ids=list(range(NCORES)))
    out = np.empty((16, S, D), np.float32)
    for c in range(NCORES):
        o = res.results[c]["oT"]
        for sq in range(NSEQ):
            out[NSEQ * c + sq] = o[sq].T
    return out


if __name__ == "__main__":
    import time
    t0 = time.time()
    nc = build_nc()
    print("instructions:", nc.n_instructions(), "build_s:", round(time.time() - t0, 1))


# version 04 (final)
# speedup vs baseline: 1.0605x; 1.0605x over previous
import contextlib
import numpy as np
import ml_dtypes
import concourse.bass as bass
import concourse.mybir as mybir
from concourse.bass_utils import run_bass_kernel_spmd

F32 = mybir.dt.float32
BF16 = mybir.dt.bfloat16
AF = mybir.ActivationFunctionType
ALU = mybir.AluOpType
AXX = mybir.AxisListType.X
bf16 = ml_dtypes.bfloat16

D = 2048
S = 2048
NSEQ = 2
NCORES = 8
TP = 512
NPASS = S // TP
DC = 16
NPAIR = 64
HC = 64
LN_EPS = 1e-5
RMS_EPS = float(np.finfo(np.float32).eps)
QTHR = float(np.float32(0.5 + 2.0 ** -9))

HEAD_A = [0, 1, 2, 3, 8, 9, 10, 11, 16, 17, 18, 19, 24, 25, 26, 27]
HEAD_PERM = []
for _m in range(16):
    HEAD_PERM += [HEAD_A[_m], HEAD_A[_m] + 4]

ENGINES = ("T", "V", "S", "G", "Y")
PHASES = ("quant", "p1", "kv", "attn", "p3")
DEBUG_SCRATCH = False
BISECT_PROJ_BARRIER = False
BISECT_GU_BARRIER = False
BISECT_H_DEDICATED = False
BISECT_LN_BARRIER = True
BISECT_LNA_BARRIER = True


class Prog:

    def __init__(self, sems):
        self.sems = sems
        self.steps = {e: [] for e in ENGINES}
        self.cnt = {k: 0 for k in sems}
        self.lastw = {}

    ESEMS = ("T", "V", "S", "G")

    def _mkwaits(self, eng, waits):
        closures = []
        for sem, val in waits:
            if val is None or val <= 0:
                continue
            v = val if sem in self.ESEMS else val * 16
            k = (eng, sem)
            if self.lastw.get(k, 0) >= v:
                continue
            self.lastw[k] = v
            h = self.sems[sem]
            closures.append(lambda e, h=h, v=v: e.wait_ge(h, v))
        return closures

    def step(self, eng, fn, waits=(), inc=True, drain=None):
        if drain is None:
            drain = eng in ("V", "G")
        closures = self._mkwaits(eng, waits)
        cnt = None
        h = None
        if inc:
            self.cnt[eng] += 1
            cnt = self.cnt[eng]
            h = self.sems[eng]

        def run(e, fn=fn, closures=closures, h=h, drain=drain):
            for c in closures:
                c(e)
            inst = fn(e)
            if h is not None:
                inst.then_inc(h, 1)
            if drain:
                e.drain()

        self.steps[eng].append(run)
        return cnt

    def dma(self, q, fn, waits=(), stream=None):
        eng = "Y" if q == "DQ" else "G"
        if stream is None:
            stream = q
        closures = self._mkwaits(eng, waits)
        self.cnt[stream] += 1
        h = self.sems[stream]

        def run(e, fn=fn, closures=closures, h=h):
            for c in closures:
                c(e)
            fn(e).then_inc(h, 16)

        self.steps[eng].append(run)
        return self.cnt[stream]

    def barrier(self):
        snap = dict(self.cnt)
        for eng in ENGINES:
            waits = [(s, snap[s]) for s in snap if s != eng]
            closures = self._mkwaits(eng, waits)
            if closures:
                self.steps[eng].append(lambda e, cs=closures: [c(e) for c in cs])


def build_nc(phases=PHASES, debug=False):
    nc = bass.Bass("TRN2", target_bir_lowering=False)

    xT = nc.dram_tensor("xT", [NSEQ, D, S], F32, kind="ExternalInput").ap()
    wgu_h = nc.dram_tensor("wgu_h", [NPAIR, 128, 4096], F32, kind="ExternalInput").ap()
    wp_h = nc.dram_tensor("wp_h", [DC, 128, 8192], F32, kind="ExternalInput").ap()
    wq_t = nc.dram_tensor("wq_t", [DC, 128, 2048], F32, kind="ExternalInput").ap()
    wkv_t = nc.dram_tensor("wkv_t", [DC, 128, 1024], F32, kind="ExternalInput").ap()
    wout_t = nc.dram_tensor("wout_t", [DC, 128, 2048], F32, kind="ExternalInput").ap()
    bq_t = nc.dram_tensor("bq_t", [128, 16], F32, kind="ExternalInput").ap()
    bkv_t = nc.dram_tensor("bkv_t", [128, 4], F32, kind="ExternalInput").ap()
    bv_r = nc.dram_tensor("bv_r", [1, 512], F32, kind="ExternalInput").ap()
    bout_t = nc.dram_tensor("bout_t", [128, 16], F32, kind="ExternalInput").ap()
    cos_h = nc.dram_tensor("cos_h", [128, S], F32, kind="ExternalInput").ap()
    sin_h = nc.dram_tensor("sin_h", [128, S], F32, kind="ExternalInput").ap()
    mask2_h = nc.dram_tensor("mask2_h", [128, 2], BF16, kind="ExternalInput").ap()
    mask2t_h = nc.dram_tensor("mask2t_h", [2, 128], BF16, kind="ExternalInput").ap()
    onesb_h = nc.dram_tensor("onesb_h", [128, 1], BF16, kind="ExternalInput").ap()
    ones1_h = nc.dram_tensor("ones1_h", [1, 128], F32, kind="ExternalInput").ap()
    ones1n_h = nc.dram_tensor("ones1n_h", [1, 128], F32, kind="ExternalInput").ap()
    onesf_h = nc.dram_tensor("onesf_h", [128, 64], F32, kind="ExternalInput").ap()
    onesb1_h = nc.dram_tensor("onesb1_h", [1, 128], BF16, kind="ExternalInput").ap()
    onesbn_h = nc.dram_tensor("onesbn_h", [1, 128], BF16, kind="ExternalInput").ap()
    onesfb_h = nc.dram_tensor("onesfb_h", [128, 64], BF16, kind="ExternalInput").ap()
    r0c_h = nc.dram_tensor("r0c_h", [128, 64], BF16, kind="ExternalInput").ap()

    skind = "ExternalOutput" if debug else "Internal"
    wgu_q = nc.dram_tensor("wgu_q", [NPAIR, 128, 4096], BF16, kind=skind).ap()
    wp_q = nc.dram_tensor("wp_q", [DC, 128, 8192], BF16, kind=skind).ap()
    y1s = nc.dram_tensor("y1s", [NSEQ, D, S], F32, kind=skind).ap()
    y2s = nc.dram_tensor("y2s", [NSEQ, D, S], F32, kind=skind).ap()
    oT = nc.dram_tensor("oT", [NSEQ, D, S], F32, kind="ExternalOutput").ap()
    if debug:
        dxn = nc.dram_tensor("dxn", [16, 128, 8192], BF16, kind="ExternalOutput").ap()
        dh = nc.dram_tensor("dh", [16, 128, 32768], BF16, kind="ExternalOutput").ap()
        drw = nc.dram_tensor("drw", [16, 16, 512], F32, kind="ExternalOutput").ap()
        dps = nc.dram_tensor("dps", [16, 128, 512], F32, kind="ExternalOutput").ap()
        dwp = nc.dram_tensor("dwp", [16, 128, 8192], BF16, kind="ExternalOutput").ap()
        dkv = nc.dram_tensor("dkv", [128, 16512], BF16, kind="ExternalOutput").ap()
        dqa = nc.dram_tensor("dqa", [2, 128, 8192], BF16, kind="ExternalOutput").ap()
        dmr = nc.dram_tensor("dmr", [8, 512], F32, kind="ExternalOutput").ap()
    else:
        dxn = dh = drw = dps = dwp = dkv = dqa = dmr = None

    with contextlib.ExitStack() as es:
        def sb(name, shape, dt):
            return es.enter_context(nc.sbuf_tensor(name, shape, dt)).ap()

        if BISECT_H_DEDICATED:
            AR1 = sb("AR1", [128, 256], F32)
            h_ded = sb("h_ded", [128, 32768], BF16)
        else:
            AR1 = sb("AR1", [128, 16384], F32)
            h_ded = None
        AR2 = sb("AR2", [128, 12288], F32)
        AR3 = sb("AR3", [128, 8192], F32)
        cos_sb = sb("cos_sb", [128, S], F32)
        sin_sb = sb("sin_sb", [128, S], F32)
        xn_sl = sb("xn_sl", [128, 8192], BF16)
        xf = [sb(f"xf{i}", [128, 512], F32) for i in range(2)]
        xb = [sb(f"xb{i}", [128, 512], BF16) for i in range(2)]
        xh = [sb(f"xh{i}", [128, 512], BF16) for i in range(2)]
        sg = [sb(f"sg{i}", [128, 512], F32) for i in range(2)]
        yo = [sb(f"yo{i}", [128, 512], F32) for i in range(2)]
        t2s = sb("t2s", [128, 512], F32)
        rbc = sb("rbc", [128, 512], F32)
        bvbc = sb("bvbc", [128, 512], F32)
        mask2 = sb("mask2", [128, 2], BF16)
        mask2t = sb("mask2t", [2, 128], BF16)
        onesb = sb("onesb", [128, 1], BF16)
        ones1 = sb("ones1", [1, 128], F32)
        ones1n = sb("ones1n", [1, 128], F32)
        onesf = sb("onesf", [128, 64], F32)
        onesb1 = sb("onesb1", [1, 128], BF16)
        onesbn = sb("onesbn", [1, 128], BF16)
        onesfb = sb("onesfb", [128, 64], BF16)
        r0c = sb("r0c", [128, 64], BF16)
        bq8 = sb("bq8", [128, 16], F32)
        bqr = sb("bqr", [128, 16], F32)
        bkv = sb("bkv", [128, 4], F32)
        bout = sb("bout", [128, 16], F32)
        negm = sb("negm", [128, 1], F32)
        ceps = sb("ceps", [128, 2], F32)

        h_sl = h_ded if BISECT_H_DEDICATED else AR1.bitcast(BF16)
        wp_b = [AR2[:, 0:4096].bitcast(BF16), AR2[:, 4096:8192].bitcast(BF16)]
        wg_b = [AR2[:, 8192:10240].bitcast(BF16), AR2[:, 10240:12288].bitcast(BF16)]

        if BISECT_H_DEDICATED:
            ARX = sb("ARX", [128, 32], F32)
            class _Z:
                def __getitem__(self, k): return ARX[0:1, 0:1]
                def bitcast(self, dt): return ARX.bitcast(dt)
                def rearrange(self, *a, **k): return ARX
            ARX_view = _Z()
        else:
            ARX = AR1
        K_sb = ARX[:, 0:4096].bitcast(BF16) if not BISECT_H_DEDICATED else ARX.bitcast(BF16)
        if BISECT_H_DEDICATED:
            V_sb = K_sb; V_v = None; E_rg = [K_sb] * 3
            krow = lambda k: ARX[0:1, 0:1]
            RK = KM = RQ = MS = RL64 = RL32 = r2_sb = mr2_sb = ARX[0:1, 0:1]
            RLH64 = RLL64 = RLH32 = RLL32 = ARX[0:1, 0:1].bitcast(BF16)
            RLBh = RLBl = ARX[0:1, 0:1].bitcast(BF16)
            RL0 = ARX[0:1, 0:1]
        else:
            V_sb = ARX[:, 4096:8256].bitcast(BF16)
            V_v = V_sb.rearrange("p (tc h v) -> p tc h v", tc=16, h=8, v=65)
            E_rg = [ARX[:, 8256 + 256 * i: 8512 + 256 * i].bitcast(BF16) for i in range(3)]
            krow = lambda k: ARX[0:1, 9024 + 512 * k: 9536 + 512 * k]
            RK, KM, RQ, MS = (krow(k) for k in range(4))
            RL64 = ARX[64:65, 9024:9536]
            RL32 = ARX[32:33, 9024:9536]
            RLH64 = ARX[64:65, 9536:10048].bitcast(BF16)[:, 0:512]
            RLL64 = ARX[64:65, 10048:10560].bitcast(BF16)[:, 0:512]
            RLH32 = ARX[32:33, 9536:10048].bitcast(BF16)[:, 0:512]
            RLL32 = ARX[32:33, 10048:10560].bitcast(BF16)[:, 0:512]
            RLBh = AR2[:, 10240:10496].bitcast(BF16)
            RLBl = AR2[:, 10496:10752].bitcast(BF16)
            RL0 = AR2[0:1, 10752:11264]
            r2_sb = ARX[0:1, 12288:14336]
            mr2_sb = ARX[0:1, 14336:16384]
        wkv_b = AR2[:, 0:8192].bitcast(BF16)
        y1_sl = AR2[:, 0:8192]
        wqo_b = [AR2[:, 8192 + 1024 * i: 9216 + 1024 * i].bitcast(BF16) for i in range(2)]
        Q_tl = AR3[:, 0:4096].bitcast(BF16)
        aT_sl = AR3[:, 4096:8192].bitcast(BF16)
        frow = lambda k: AR3[0:1, 512 * k: 512 * (k + 1)]

        q_in = [AR2[:, 0:2048].bitcast(BF16), AR2[:, 2048:4096].bitcast(BF16)]
        q_out = [AR2[:, 4096:6144].bitcast(BF16), AR2[:, 6144:8192].bitcast(BF16)]
        q_th = AR2[:, 8192:12288]
        q_aw = AR3[:, 0:2048].bitcast(BF16)
        q_qv = AR3[:, 2048:4096].bitcast(BF16)
        q_c = AR3[:, 4096:6144].bitcast(BF16)
        q_sbc = AR3[:, 6144:8192].bitcast(BF16)
        q_sc = ([ARX[0:2, 256 * i: 256 * (i + 1)].bitcast(BF16) for i in range(2)]
                if not BISECT_H_DEDICATED else [xb[0][0:2, :], xb[1][0:2, :]])

        PB = [es.enter_context(nc.psum_tensor(f"PB{i}", [128, 512], F32)).ap()
              for i in range(8)]

        sems = {k: es.enter_context(nc.semaphore(k)) for k in
                ("T", "V", "S", "G", "DQ", "GQ",
                 "WG", "WP", "XL", "YS", "WA", "KVL", "QS", "QO")}
        p = Prog(sems)

        def bcast2(bank, row_f32, hi, lo, wV, neg=False, lhs_hi=None, lhs_lo=None,
                   tile_position=None):
            ch = p.step("V", lambda e: e.tensor_copy(hi, row_f32), waits=[("V", wV)])
            cl = p.step("V", lambda e: e.tensor_tensor(lo, row_f32, hi,
                                                       op=ALU.subtract))
            l1 = lhs_hi if lhs_hi is not None else (onesbn if neg else onesb1)[:, :]
            l2 = lhs_lo if lhs_lo is not None else l1
            def mms(e, bank=bank, hi=hi, lo=lo, l1=l1, l2=l2, tp=tile_position):
                kw = {} if tp is None else dict(tile_position=tp)
                e.matmul(bank, l1, hi, start=True, stop=False, **kw)
                return e.matmul(bank, l2, lo, start=False, stop=True, **kw)
            return p.step("T", mms, waits=[("V", cl)])


        for dst, src in [(cos_sb[:], cos_h[:, :]), (sin_sb[:], sin_h[:, :]),
                         (mask2[:], mask2_h[:, :]), (mask2t[:], mask2t_h[:, :]),
                         (onesb[:], onesb_h[:, :]), (ones1[:], ones1_h[:, :]),
                         (ones1n[:], ones1n_h[:, :]), (onesf[:], onesf_h[:, :]),
                         (onesb1[:], onesb1_h[:, :]), (onesbn[:], onesbn_h[:, :]),
                         (onesfb[:], onesfb_h[:, :]), (r0c[:], r0c_h[:, :]),
                         (bqr[:], bq_t[:, :]),
                         (bkv[:], bkv_t[:, :]), (bout[:], bout_t[:, :])]:
            p.dma("DQ", lambda e, dst=dst, src=src: e.dma_start(out=dst, in_=src))
        ld0 = p.cnt["DQ"]
        p.step("S", lambda e: e.activation(bq8[:], bqr[:], AF.Copy, scale=0.125),
               waits=[("DQ", ld0)])
        ldbv = p.dma("DQ", lambda e: e.dma_start(out=yo[0][0:1, :], in_=bv_r[:, :]))
        cwv = p.step("V", lambda e: e.tensor_copy(yo[0][0:1, 0:1], yo[0][0:1, 0:1]),
                     waits=[("DQ", ldbv)])
        cT = bcast2(PB[0][:], yo[0][0:1, :],
                    yo[1][0:1, :].bitcast(BF16)[:, 0:512],
                    t2s[0:1, :].bitcast(BF16)[:, 0:512], cwv)
        p.step("V", lambda e: e.tensor_copy(bvbc[:], PB[0][:]), waits=[("T", cT)])
        p.step("V", lambda e: e.memset(ceps[:, 0:1], LN_EPS))
        p.step("V", lambda e: e.memset(ceps[:, 1:2], RMS_EPS))
        p.barrier()

        qblocks = [(wgu_h[g], wgu_q[g]) for g in range(NPAIR)]
        for d in range(DC):
            qblocks.append((wp_h[d, :, 0:4096], wp_q[d, :, 0:4096]))
            qblocks.append((wp_h[d, :, 4096:8192], wp_q[d, :, 4096:8192]))

        if "quant" in phases:
            lastVo = [0, 0]
            lastGd = [0, 0]
            lastS = [0, 0]
            lastst = [0, 0]
            pV1 = [0, 0]
            pS2 = [0, 0]
            pVa, pGb, pVw, pGd, pVo, pSaw = 0, 0, 0, 0, 0, 0
            for b, (srcb, dstb) in enumerate(qblocks):
                bi = b % 2
                ld = p.dma("GQ", lambda e, srcb=srcb, bi=bi:
                           e.dma_start(out=q_in[bi][:, :], in_=srcb),
                           waits=[("V", lastVo[bi]), ("G", lastGd[bi]),
                                  ("S", lastS[bi])], stream="QS")
                cS_aw = p.step("S", lambda e, bi=bi:
                               e.activation(q_aw[:, :], q_in[bi][:, :], AF.Abs),
                               waits=[("QS", ld), ("V", pVo), ("G", pGd)])
                lastS[bi] = cS_aw
                cT1 = 0
                cS1 = cS2 = 0
                for j in range(8):
                    qi = j % 2
                    sl = slice(512 * j, 512 * (j + 1))
                    cT1 = p.step("T", lambda e, sl=sl, qi=qi:
                                 e.matmul(PB[qi][0:2, :], mask2[:], q_aw[:, sl],
                                          start=True, stop=True),
                                 waits=[("S", cS_aw), ("V", pV1[qi])])
                    cV1 = p.step("V", lambda e, qi=qi:
                                 e.tensor_scalar(q_sc[qi][:, :], PB[qi][0:2, :],
                                                 1.0 / 64, 1e-8,
                                                 op0=ALU.mult, op1=ALU.max),
                                 waits=[("T", cT1)])
                    pV1[qi] = cV1
                    cT2 = p.step("T", lambda e, qi=qi:
                                 e.matmul(PB[2 + qi][:], mask2t[:], q_sc[qi][:, :],
                                          start=True, stop=True),
                                 waits=[("V", cV1), ("S", pS2[qi])])
                    cS1 = p.step("S", lambda e, sl=sl, qi=qi:
                                 e.activation(q_th[:, sl], PB[2 + qi][:], AF.Copy,
                                              scale=QTHR),
                                 waits=[("T", cT2), ("V", pGb)])
                    cS2 = p.step("S", lambda e, sl=sl, qi=qi:
                                 e.activation(q_sbc[:, sl], PB[2 + qi][:], AF.Copy),
                                 waits=[("V", pVw)])
                    pS2[qi] = cS2
                cVa = p.step("V", lambda e, bi=bi:
                             e.tensor_tensor(q_c[:, :], q_in[bi][:, :], q_th[:, :],
                                             op=ALU.is_gt),
                             waits=[("S", cS2), ("G", pGd)])
                cVb = p.step("V", lambda e, bi=bi:
                             e.scalar_tensor_tensor(q_aw[:, :], q_in[bi][:, :], -1.0,
                                                    q_th[:, :], op0=ALU.mult,
                                                    op1=ALU.is_gt),
                             waits=[("T", cT1)])
                cGq = p.step("G", lambda e:
                             e.tensor_tensor(q_qv[:, :], q_c[:, :], q_aw[:, :],
                                             op=ALU.subtract),
                             waits=[("V", cVb)])
                cVw = p.step("V", lambda e:
                             e.tensor_tensor(q_c[:, :], q_qv[:, :], q_sbc[:, :],
                                             op=ALU.mult),
                             waits=[("G", cGq)])
                cGd = p.step("G", lambda e, bi=bi:
                             e.tensor_tensor(q_aw[:, :], q_c[:, :], q_in[bi][:, :],
                                             op=ALU.subtract),
                             waits=[("V", cVw)])
                cVo = p.step("V", lambda e, bi=bi:
                             e.tensor_tensor(q_out[bi][:, :], q_in[bi][:, :],
                                             q_aw[:, :], op=ALU.add),
                             waits=[("G", cGd), ("QO", lastst[bi])])
                lastVo[bi] = cVo
                lastGd[bi] = cGd
                pVa, pGb, pVw, pGd, pVo, pSaw = cVa, cVb, cVw, cGd, cVo, cS_aw
                lastst[bi] = p.dma("DQ", lambda e, dstb=dstb, bi=bi:
                                   e.dma_start(out=dstb, in_=q_out[bi][:, :]),
                                   waits=[("V", cVo)], stream="QO")
            p.barrier()

        def stats_chain(rows, sum_ps, sq_ps, n, eps, eps_ap, r_out, mr_out, wT):
            m, e2, t, v, vep, s0, r0 = (rows(k) for k in range(7))
            p.step("V", lambda e: e.tensor_single_scalar(m, sum_ps, 1.0 / n,
                                                         op=ALU.mult), waits=[("T", wT)])
            p.step("V", lambda e: e.tensor_single_scalar(e2, sq_ps, 1.0 / n,
                                                         op=ALU.mult))
            p.step("V", lambda e: e.tensor_tensor(t, m, m, op=ALU.mult))
            p.step("V", lambda e: e.tensor_tensor(v, e2, t, op=ALU.subtract))
            ce = p.step("V", lambda e: e.tensor_single_scalar(vep, v, eps, op=ALU.add))
            cs = p.step("S", lambda e: e.activation(s0, v, AF.Sqrt, bias=eps_ap),
                        waits=[("V", ce)])
            p.step("V", lambda e: e.reciprocal(r0, s0), waits=[("S", cs)])
            p.step("V", lambda e: e.tensor_tensor(t, r0, r0, op=ALU.mult))
            p.step("V", lambda e: e.tensor_tensor(t, t, vep, op=ALU.mult))
            p.step("V", lambda e: e.tensor_scalar(t, t, -0.5, 1.5, op0=ALU.mult,
                                                  op1=ALU.add))
            cr = p.step("V", lambda e: e.tensor_tensor(r_out, r0, t, op=ALU.mult))
            if mr_out is not None:
                cr = p.step("V", lambda e: e.tensor_tensor(mr_out, m, r_out,
                                                           op=ALU.mult))
            return cr

        def rms_chain(rows, sq_ps, n, wT):
            t, v, vep, s0, r0, r1 = (rows(k) for k in range(6))
            cv = p.step("V", lambda e: e.tensor_single_scalar(v, sq_ps, 1.0 / n,
                                                              op=ALU.mult),
                        waits=[("T", wT)])
            p.step("V", lambda e: e.tensor_single_scalar(vep, v, RMS_EPS, op=ALU.add))
            cs = p.step("S", lambda e: e.activation(s0, v, AF.Sqrt,
                                                    bias=ceps[0:1, 1:2]),
                        waits=[("V", cv)])
            p.step("V", lambda e: e.reciprocal(r0, s0), waits=[("S", cs)])
            p.step("V", lambda e: e.tensor_tensor(t, r0, r0, op=ALU.mult))
            p.step("V", lambda e: e.tensor_tensor(t, t, vep, op=ALU.mult))
            p.step("V", lambda e: e.tensor_scalar(t, t, -0.5, 1.5, op0=ALU.mult,
                                                  op1=ALU.add))
            cr = p.step("V", lambda e: e.tensor_tensor(r1, r0, t, op=ALU.mult))
            return cr, r1

        def ln_stats(src2d, t0):
            uV = [0, 0]
            uS = [0, 0]
            uT = [0, 0]
            cT = None
            for dc in range(DC):
                i = dc % 2
                ld = p.dma("DQ", lambda e, dc=dc, i=i, src2d=src2d, t0=t0:
                           e.dma_start(out=xf[i][:, :],
                                       in_=src2d[128 * dc:128 * (dc + 1), t0:t0 + 512]),
                           waits=[("V", uV[i]), ("S", uS[i])], stream="XL")
                uV[i] = p.step("V", lambda e, i=i:
                               e.tensor_copy(xb[i][:, :], xf[i][:, :]),
                               waits=[("XL", ld), ("T", uT[i])])
                uS[i] = p.step("S", lambda e, i=i:
                               e.activation(xh[i][:, :], xf[i][:, :], AF.Square),
                               waits=[("XL", ld), ("T", uT[i])])
                p.step("T", lambda e, i=i, dc=dc:
                       e.matmul(PB[6][0:1, :], onesb[:], xb[i][:, :],
                                start=(dc == 0), stop=(dc == DC - 1)),
                       waits=[("V", uV[i])], inc=False)
                cT = p.step("T", lambda e, i=i, dc=dc:
                            e.matmul(PB[7][0:1, :], onesb[:], xh[i][:, :],
                                     start=(dc == 0), stop=(dc == DC - 1)),
                            waits=[("S", uS[i])])
                uT[i] = cT
                if BISECT_LN_BARRIER:
                    p.barrier()
            return cT, uV, uS

        def ln_apply(src2d, t0, cr, r_row, mr_row, uV0, uS0):
            bcast2(PB[6][:], r_row, frow(10).bitcast(BF16)[:, 0:512],
                   frow(11).bitcast(BF16)[:, 0:512], cr)
            cb2 = bcast2(PB[7][:], mr_row, frow(12).bitcast(BF16)[:, 0:512],
                         frow(13).bitcast(BF16)[:, 0:512], None, neg=True)
            uV = list(uV0)
            uS = list(uS0)
            lastV = None
            for dc in range(DC):
                i = dc % 2
                ld = p.dma("DQ", lambda e, dc=dc, i=i, src2d=src2d, t0=t0:
                           e.dma_start(out=xf[i][:, :],
                                       in_=src2d[128 * dc:128 * (dc + 1), t0:t0 + 512]),
                           waits=[("V", uV[i]), ("S", uS[i])], stream="XL")
                uS[i] = 0
                p.step("V", lambda e, i=i: e.tensor_tensor(sg[i][:, :], xf[i][:, :],
                                                           PB[6][:], op=ALU.mult),
                       waits=[("XL", ld), ("T", cb2)])
                lastV = p.step("V", lambda e, i=i, dc=dc:
                               e.tensor_tensor(xn_sl[:, 512 * dc:512 * (dc + 1)],
                                               sg[i][:, :], PB[7][:], op=ALU.add))
                uV[i] = lastV
                if BISECT_LN_BARRIER or BISECT_LNA_BARRIER:
                    p.barrier()
            return lastV

        passno = [0]

        def ffn_pass(src2d, dst2d, t0):
            pn = passno[0]
            passno[0] += 1
            p.barrier()
            cT_st, uV0, uS0 = ln_stats(src2d, t0)
            cr = stats_chain(frow, PB[6][0:1, :], PB[7][0:1, :], D, LN_EPS,
                             ceps[0:1, 0:1], frow(8), frow(9), cT_st)
            xn_done = ln_apply(src2d, t0, cr, frow(8), frow(9), uV0, uS0)
            if debug and dxn is not None and pn < 16:
                p.dma("DQ", lambda e, pn=pn: e.dma_start(out=dxn[pn], in_=xn_sl[:, :]),
                      waits=[("V", xn_done)])
                for k in range(10):
                    p.dma("DQ", lambda e, pn=pn, k=k:
                          e.dma_start(out=drw[pn, k:k + 1, :], in_=frow(k)),
                          waits=[("V", cr)])
            p.barrier()
            uS = [0, 0]
            uVh = [0, 0]
            uTn = [0, 0]
            lastTgu = [0, 0]
            ss_last = None
            h_done = None
            ss_pend = None

            def emit_ss(gp, ip, csqp):
                return p.step("T", lambda e, ip=ip, gp=gp:
                              e.matmul(PB[6][0:1, :], onesb[:], xh[ip][:, :],
                                       start=(gp == 0), stop=(gp == NPAIR - 1)),
                              waits=[("S", csqp)])

            for g in range(NPAIR):
                i = g % 2
                pg, pu = PB[2 + 2 * i], PB[3 + 2 * i]
                ld = p.dma("DQ", lambda e, g=g, i=i:
                           e.dma_start(out=wg_b[i][:, :], in_=wgu_q[g]),
                           waits=[("T", lastTgu[i])], stream="WG")

                for sub in range(2):
                    def gu_mms(e, i=i, pg=pg, pu=pu, sub=sub):
                        last = None
                        tgt, off = (pg, 0) if sub == 0 else (pu, 2048)
                        for dc in range(DC):
                            last = e.matmul(tgt[:],
                                            wg_b[i][:, off + 128 * dc:off + 128 * (dc + 1)],
                                            xn_sl[:, 512 * dc:512 * (dc + 1)],
                                            start=(dc == 0), stop=(dc == DC - 1))
                        return last

                    lastTgu[i] = p.step("T", gu_mms,
                                        waits=[("WG", ld), ("V", xn_done),
                                               ("S", uS[i]), ("V", uVh[i])])
                if ss_pend is not None:
                    gp, ip, csqp = ss_pend
                    ss_last = uTn[ip] = emit_ss(gp, ip, csqp)
                uS[i] = p.step("S", lambda e, i=i, pg=pg:
                               e.activation(sg[i][:, :], pg[:], AF.Silu),
                               waits=[("T", lastTgu[i]), ("V", uVh[i])])
                h_done = uVh[i] = p.step("V", lambda e, i=i, g=g, pu=pu:
                                         e.tensor_tensor(h_sl[:, 512 * g:512 * (g + 1)],
                                                         sg[i][:, :], pu[:],
                                                         op=ALU.mult),
                                         waits=[("S", uS[i])])
                cSq = p.step("S", lambda e, i=i, g=g:
                             e.activation(xh[i][:, :],
                                          h_sl[:, 512 * g:512 * (g + 1)], AF.Square),
                             waits=[("V", uVh[i]), ("T", uTn[i])])
                ss_pend = (g, i, cSq)
                if BISECT_GU_BARRIER:
                    p.barrier()
            gp, ip, csqp = ss_pend
            ss_last = uTn[ip] = emit_ss(gp, ip, csqp)
            if debug and dh is not None and pn < 16:
                p.dma("DQ", lambda e, pn=pn: e.dma_start(out=dh[pn], in_=h_sl[:, :]),
                      waits=[("V", h_done)])
            crh, rh = rms_chain(frow, PB[6][0:1, :], 8192, ss_last)
            cbr = bcast2(PB[7][:], rh, frow(10).bitcast(BF16)[:, 0:512],
                         frow(11).bitcast(BF16)[:, 0:512], crh)
            cvr = p.step("V", lambda e: e.tensor_copy(rbc[:], PB[7][:]),
                         waits=[("T", cbr)])
            if debug and drw is not None and pn < 16:
                for k in range(10, 16):
                    p.dma("DQ", lambda e, pn=pn, k=k:
                          e.dma_start(out=drw[pn, k:k + 1, :], in_=frow(k - 10)),
                          waits=[("V", crh)])
            lastTp = [0, 0]
            uVu = [0, 0]
            uVy = [0, 0]
            stc = [0, 0]
            for dcp in range(DC):
                i = dcp % 2
                pp = PB[i]
                ldw = p.dma("DQ", lambda e, dcp=dcp, i=i:
                            e.dma_start(out=wp_b[i][:, :], in_=wp_q[dcp]),
                            waits=[("T", lastTp[i])], stream="WP")
                ldx = p.dma("DQ", lambda e, dcp=dcp, i=i, src2d=src2d, t0=t0:
                            e.dma_start(out=xf[i][:, :],
                                        in_=src2d[128 * dcp:128 * (dcp + 1), t0:t0 + 512]),
                            waits=[("V", uVy[i])], stream="XL")

                for sub in range(4):
                    def proj_mms(e, i=i, pp=pp, sub=sub):
                        last = None
                        for hc in range(16 * sub, 16 * (sub + 1)):
                            last = e.matmul(pp[:], wp_b[i][:, 128 * hc:128 * (hc + 1)],
                                            h_sl[:, 512 * hc:512 * (hc + 1)],
                                            start=(hc == 0), stop=(hc == HC - 1))
                        return last

                    lastTp[i] = p.step("T", proj_mms,
                                       waits=[("WP", ldw), ("V", uVu[i]),
                                              ("V", h_done)])
                uVu[i] = p.step("V", lambda e, i=i, pp=pp:
                                e.tensor_tensor(sg[i][:, :], pp[:], rbc[:],
                                                op=ALU.mult),
                                waits=[("T", lastTp[i]), ("V", cvr)])
                uVy[i] = p.step("V", lambda e, i=i:
                                e.scalar_tensor_tensor(yo[i][:, :], sg[i][:, :], 0.5,
                                                       xf[i][:, :], op0=ALU.mult,
                                                       op1=ALU.add),
                                waits=[("XL", ldx), ("YS", stc[i])])
                stc[i] = p.dma("DQ", lambda e, dcp=dcp, i=i, dst2d=dst2d, t0=t0:
                               e.dma_start(out=dst2d[128 * dcp:128 * (dcp + 1),
                                                     t0:t0 + 512],
                                           in_=yo[i][:, :]),
                               waits=[("V", uVy[i])], stream="YS")
                if debug and dps is not None and pn == 0:
                    cpe = p.step("S", lambda e, i=i, pp=pp:
                                 e.activation(t2s[:], pp[:], AF.Copy),
                                 waits=[("V", uVu[i]), ("T", lastTp[i])])
                    p.dma("DQ", lambda e, dcp=dcp: e.dma_start(out=dps[dcp],
                                                               in_=t2s[:, :]),
                          waits=[("S", cpe)])
                    p.dma("DQ", lambda e, dcp=dcp, i=i:
                          e.dma_start(out=dwp[dcp], in_=wp_b[i][:, :]),
                          waits=[("T", lastTp[i])])
                    p.barrier()
                if BISECT_PROJ_BARRIER:
                    p.barrier()

        def rope_chunk(src_f32, dst_bf, t0):
            c1 = p.step("V", lambda e: e.stream_shuffle(t2s[:], src_f32,
                                                        [i ^ 1 for i in range(32)]))
            cg = p.step("G", lambda e, t0=t0:
                        e.tensor_tensor(t2s[:], t2s[:], sin_sb[:, t0:t0 + 512],
                                        op=ALU.mult),
                        waits=[("V", c1)])
            p.step("V", lambda e, t0=t0:
                   e.tensor_tensor(sg[1][:, :], src_f32, cos_sb[:, t0:t0 + 512],
                                   op=ALU.mult))
            cv = p.step("V", lambda e, dst_bf=dst_bf:
                        e.tensor_tensor(dst_bf, sg[1][:, :], t2s[:], op=ALU.add),
                        waits=[("G", cg)])
            return cv

        for s in range(NSEQ):
            xT_s, y1_s, y2_s, oT_s = xT[s], y1s[s], y2s[s], oT[s]

            if "p1" in phases:
                for pi in range(NPASS):
                    ffn_pass(xT_s, y1_s, pi * TP)
                p.barrier()

            if "kv" in phases:
                p.barrier()
                p.step("V", lambda e: e.memset(V_v[:, :, :, 64:65], 1.0))
                p.step("V", lambda e: e.memset(RK, 0.0))
                ldkv = p.dma("GQ", lambda e: e.dma_start(
                    out=wkv_b[:].rearrange("p (c q) -> p c q", q=1024),
                    in_=wkv_t[:, :, :].rearrange("c p q -> p c q")),
                    stream="KVL")
                for blk in range(NPASS):
                    t0 = blk * TP
                    p.barrier()
                    cT_st, uV0, uS0 = ln_stats(y1_s, t0)
                    cr2 = stats_chain(frow, PB[6][0:1, :], PB[7][0:1, :], D, LN_EPS,
                                      ceps[0:1, 0:1], r2_sb[0:1, t0:t0 + 512],
                                      mr2_sb[0:1, t0:t0 + 512], cT_st)
                    xn2_done = ln_apply(y1_s, t0, cr2, r2_sb[0:1, t0:t0 + 512],
                                        mr2_sb[0:1, t0:t0 + 512], uV0, uS0)
                    p.barrier()
                    pnn = 0
                    pVmax = [0]
                    pkT = [(0, 0), (0, 0)]
                    for kc in range(4):
                        i = kc % 2
                        pk = PB[2 + i]

                        def k_mms(e, kc=kc, pk=pk):
                            last = None
                            for dc in range(DC):
                                last = e.matmul(pk[:],
                                                wkv_b[:, 1024 * dc + 128 * kc:
                                                      1024 * dc + 128 * (kc + 1)],
                                                xn_sl[:, 512 * dc:512 * (dc + 1)],
                                                start=(dc == 0), stop=(dc == DC - 1))
                            return last

                        cTk = p.step("T", k_mms,
                                     waits=[("KVL", ldkv), ("V", xn2_done),
                                            ("S", pkT[i][0]), ("V", pkT[i][1])])
                        cVk = p.step("V", lambda e, kc=kc, pk=pk:
                                     e.tensor_single_scalar(sg[0][:, :], pk[:],
                                                            bkv[:, kc:kc + 1],
                                                            op=ALU.add),
                                     waits=[("T", cTk)])
                        cSq = p.step("S", lambda e, kc=kc, pk=pk:
                                     e.activation(xh[0][:, :], pk[:], AF.Square,
                                                  bias=bkv[:, kc:kc + 1]),
                                     waits=[("T", pnn), ("V", cVk)])
                        pkT[i] = (cSq, cVk)
                        pnn = p.step("T", lambda e: e.matmul(PB[6][0:1, :], onesb[:],
                                                             xh[0][:, :],
                                                             start=True, stop=True),
                                     waits=[("S", cSq), ("V", pVmax[0])])
                        pVmax[0] = p.step("V", lambda e, pnn=pnn:
                                          e.tensor_tensor(RK, RK, PB[6][0:1, :],
                                                          op=ALU.max),
                                          waits=[("T", pnn)])
                        rope_chunk(sg[0][:, :],
                                   K_sb[:, 2048 * kc + t0: 2048 * kc + t0 + 512], t0)
                    pvV = [0, 0]
                    for tb in range(4):
                        i = tb % 2
                        pv = PB[4 + i]

                        def v_mms(e, tb=tb, pv=pv):
                            last = None
                            for dc in range(DC):
                                last = e.matmul(pv[:],
                                                xn_sl[:, 512 * dc + 128 * tb:
                                                      512 * dc + 128 * (tb + 1)],
                                                wkv_b[:, 1024 * dc + 512:
                                                      1024 * (dc + 1)],
                                                start=(dc == 0), stop=(dc == DC - 1))
                            return last

                        cTv = p.step("T", v_mms, waits=[("V", pvV[i])])
                        tc = blk * 4 + tb
                        pvV[i] = p.step("V", lambda e, tc=tc, pv=pv:
                                        e.tensor_tensor(
                                            V_v[:, tc, :, 0:64],
                                            pv[:].rearrange("p (h v) -> p h v", v=64),
                                            bvbc[:].rearrange("p (h v) -> p h v", v=64),
                                            op=ALU.add),
                                        waits=[("T", cTv)])
                p.step("V", lambda e: e.reduce_max(KM[:, 0:1], RK, axis=AXX))
                p.barrier()
                if debug and dkv is not None and s == 0:
                    p.dma("DQ", lambda e: e.dma_start(out=dkv[:, 0:8192], in_=K_sb[:, :]))
                    p.dma("DQ", lambda e: e.dma_start(out=dkv[:, 8192:16512],
                                                      in_=V_sb[:, :]))
                    p.dma("DQ", lambda e: e.dma_start(out=dmr[0:1, :], in_=RK))
                    p.dma("DQ", lambda e: e.dma_start(out=dmr[1:2, :], in_=KM))
                    p.dma("DQ", lambda e: e.dma_start(out=dmr[2:3, :],
                                                      in_=r2_sb[0:1, 0:512]))
                    p.dma("DQ", lambda e: e.dma_start(out=dmr[3:4, :],
                                                      in_=mr2_sb[0:1, 0:512]))
                    p.barrier()

            if "attn" in phases:
                p.step("V", lambda e: e.memset(RLBh[:, :], 0.0))
                p.step("V", lambda e: e.memset(RLBl[:, :], 0.0))
                for qt in range(NPASS):
                    q0 = qt * TP
                    p.barrier()
                    y1lds = []
                    for dc in range(DC):
                        y1lds.append(p.dma("DQ", lambda e, dc=dc, y1_s=y1_s, q0=q0:
                                           e.dma_start(
                                               out=y1_sl[:, 512 * dc:512 * (dc + 1)],
                                               in_=y1_s[128 * dc:128 * (dc + 1),
                                                        q0:q0 + 512]),
                                           stream="XL"))
                    cbA = bcast2(PB[6][:], r2_sb[0:1, q0:q0 + 512],
                                 krow(4).bitcast(BF16)[:, 0:512],
                                 krow(5).bitcast(BF16)[:, 0:512], None)
                    p.step("V", lambda e: e.tensor_copy(MS[:, 8:9], MS[:, 8:9]),
                           waits=[("T", cbA)])
                    cb2 = bcast2(PB[7][:], mr2_sb[0:1, q0:q0 + 512],
                                 krow(4).bitcast(BF16)[:, 0:512],
                                 krow(5).bitcast(BF16)[:, 0:512], None, neg=True)
                    xn2_done = None
                    for dc in range(DC):
                        p.step("V", lambda e, dc=dc:
                               e.tensor_tensor(sg[dc % 2][:, :],
                                               y1_sl[:, 512 * dc:512 * (dc + 1)],
                                               PB[6][:], op=ALU.mult),
                               waits=[("XL", y1lds[dc]), ("T", cb2)])
                        xn2_done = p.step("V", lambda e, dc=dc:
                                          e.tensor_tensor(
                                              xn_sl[:, 512 * dc:512 * (dc + 1)],
                                              sg[dc % 2][:, :], PB[7][:], op=ALU.add))
                    p.step("V", lambda e: e.memset(RQ, 0.0))
                    p.barrier()
                    lastTq = [0, 0]
                    uSq2 = [0, 0]
                    uVq = [0, 0]
                    pnn = 0
                    pVmax = [0]
                    for qc in range(16):
                        i = qc % 2
                        pq = PB[i]
                        ldw = p.dma("GQ", lambda e, qc=qc, i=i:
                                    e.dma_start(out=wqo_b[i][:].rearrange(
                                        "p (c q) -> p c q", q=128),
                                        in_=wq_t[:, :, 128 * qc:128 * (qc + 1)].rearrange(
                                            "c p q -> p c q")),
                                    waits=[("T", lastTq[i])], stream="WA")

                        def q_mms(e, i=i, pq=pq):
                            last = None
                            for dc in range(DC):
                                last = e.matmul(pq[:],
                                                wqo_b[i][:, 128 * dc:128 * (dc + 1)],
                                                xn_sl[:, 512 * dc:512 * (dc + 1)],
                                                start=(dc == 0), stop=(dc == DC - 1))
                            return last

                        lastTq[i] = p.step("T", q_mms,
                                           waits=[("WA", ldw), ("V", xn2_done),
                                                  ("S", uSq2[i]), ("V", uVq[i])])
                        p.step("V", lambda e, qc=qc, pq=pq:
                               e.tensor_single_scalar(sg[0][:, :], pq[:],
                                                      bqr[:, qc:qc + 1],
                                                      op=ALU.add),
                               waits=[("T", lastTq[i])])
                        uVq[i] = p.step("V", lambda e:
                                        e.tensor_single_scalar(yo[0][:, :],
                                                               sg[0][:, :], 0.125,
                                                               op=ALU.mult))
                        uSq2[i] = p.step("S", lambda e, qc=qc, pq=pq:
                                         e.activation(xh[0][:, :], pq[:], AF.Square,
                                                      bias=bq8[:, qc:qc + 1],
                                                      scale=0.125),
                                         waits=[("T", pnn), ("V", uVq[i])])
                        pnn = p.step("T", lambda e: e.matmul(PB[6][0:1, :], onesb[:],
                                                             xh[0][:, :],
                                                             start=True, stop=True),
                                     waits=[("S", uSq2[i]), ("V", pVmax[0])])
                        pVmax[0] = p.step("V", lambda e, pnn=pnn:
                                          e.tensor_tensor(RQ, RQ, PB[6][0:1, :],
                                                          op=ALU.max),
                                          waits=[("T", pnn)])
                        rope_chunk(yo[0][:, :], Q_tl[:, 512 * qc:512 * (qc + 1)], q0)
                    p.step("V", lambda e: e.reduce_max(MS[:, 0:1], RQ, axis=AXX))
                    cvm = p.step("V", lambda e: e.tensor_tensor(MS[:, 1:2], MS[:, 0:1],
                                                                KM[:, 0:1], op=ALU.mult))
                    csm = p.step("S", lambda e: e.activation(MS[:, 2:3], MS[:, 1:2],
                                                             AF.Sqrt),
                                 waits=[("V", cvm)])
                    cvn = p.step("V", lambda e: e.tensor_single_scalar(
                        MS[:, 3:4], MS[:, 2:3], -1.0, op=ALU.mult), waits=[("S", csm)])
                    cvn = p.step("V", lambda e: e.tensor_copy(
                        MS.bitcast(BF16)[:, 8:9], MS[:, 3:4]))
                    ctm = p.step("T", lambda e: e.matmul(PB[6][:, 0:1], onesb1[:, :],
                                                         MS.bitcast(BF16)[:, 8:9],
                                                         start=True, stop=True),
                                 waits=[("V", cvn)])
                    cvb = p.step("V", lambda e: e.tensor_copy(negm[:, 0:1],
                                                              PB[6][:, 0:1]),
                                 waits=[("T", ctm)])
                    p.barrier()
                    uExpBank = [0, 0, 0, 0]
                    pcrl1 = [0]
                    uPV = {0: 0, 1: 0, 2: 0}
                    lastVat = [0, 0]
                    pcvrl = 0
                    for hh in range(32):
                        m, slot = hh // 2, hh % 2
                        head = HEAD_PERM[hh]
                        g = head // 4
                        kc = head // 8
                        off = 64 * slot
                        pat = PB[5 + 2 * (hh % 2)]
                        for kb in range(16):
                            pss = (PB[3], PB[4], PB[0], PB[1])[kb % 4]
                            cqk = p.step("T",
                                         lambda e, kb=kb, kc=kc, off=off, m=m, pss=pss:
                                         e.matmul(pss[:],
                                                  K_sb[off:off + 64,
                                                       2048 * kc + 128 * kb:
                                                       2048 * kc + 128 * (kb + 1)],
                                                  Q_tl[off:off + 64,
                                                       512 * m:512 * (m + 1)],
                                                  start=True, stop=True,
                                                  tile_position=(off, 0)),
                                         waits=[("V", cvb), ("S", uExpBank[kb % 4])])
                            cexp = p.step("S", lambda e, kb=kb, pss=pss:
                                          e.activation(E_rg[kb % 3][:, :], pss[:],
                                                       AF.Exp, bias=negm[:, 0:1]),
                                          waits=[("T", cqk), ("T", uPV[kb % 3])])
                            uExpBank[kb % 4] = cexp
                            if slot == 0:
                                uPV[kb % 3] = p.step(
                                    "T", lambda e, kb=kb, g=g, pat=pat:
                                    e.matmul(pat[0:65, :], V_v[:, kb, g, :],
                                             E_rg[kb % 3][:, :],
                                             start=(kb == 0), stop=(kb == 15)),
                                    waits=[("S", cexp), ("V", lastVat[hh % 2])])
                            else:
                                p.step("T", lambda e, kb=kb, g=g, pat=pat:
                                       e.matmul(pat[64:128, :],
                                                V_v[:, kb, g, 0:64],
                                                E_rg[kb % 3][:, :],
                                                start=(kb == 0), stop=(kb == 15),
                                                tile_position=(0, 64)),
                                       waits=[("S", cexp), ("V", lastVat[hh % 2])],
                                       inc=False)
                                uPV[kb % 3] = p.step(
                                    "T", lambda e, kb=kb:
                                    e.matmul(PB[2][0:1, :], onesb[:],
                                             E_rg[kb % 3][:, :],
                                             start=(kb == 0), stop=(kb == 15)),
                                    waits=[("V", pcrl1[0])] if kb == 0 else ())
                        if slot == 0:
                            crl = p.step("V", lambda e, pat=pat:
                                         e.reciprocal(RL64, pat[64:65, :]),
                                         waits=[("T", uPV[15 % 3])])
                            ctrl = bcast2(PB[6][0:64, :], RL64, RLH64, RLL64, pcvrl,
                                          lhs_hi=onesfb[64:65, 0:64],
                                          tile_position=(64, 0))
                            pcvrl = p.step("V", lambda e, i=hh % 2:
                                           e.tensor_copy(xf[i][0:64, :],
                                                         PB[6][0:64, :]),
                                           waits=[("T", ctrl)])
                            lastVat[hh % 2] = p.step(
                                "V", lambda e, pat=pat, i=hh % 2, m=m:
                                e.tensor_tensor(aT_sl[0:64, 512 * m:512 * (m + 1)],
                                                pat[0:64, :], xf[i][0:64, :],
                                                op=ALU.mult))
                        else:
                            crl = p.step("V", lambda e:
                                         e.reciprocal(RL0, PB[2][0:1, :]),
                                         waits=[("T", uPV[15 % 3])])
                            pcrl1[0] = crl
                            p.step("V", lambda e: e.tensor_copy(RLBh[0:1, :], RL0),
                                   waits=[("V", pcvrl)])
                            cl2 = p.step("V", lambda e:
                                         e.tensor_tensor(RLBl[0:1, :], RL0,
                                                         RLBh[0:1, :],
                                                         op=ALU.subtract))
                            def rl1_mms(e):
                                e.matmul(PB[6][64:128, :], r0c[:], RLBh[:, :],
                                         start=True, stop=False,
                                         tile_position=(0, 64))
                                return e.matmul(PB[6][64:128, :], r0c[:], RLBl[:, :],
                                                start=False, stop=True,
                                                tile_position=(0, 64))
                            ctrl = p.step("T", rl1_mms, waits=[("V", cl2)])
                            pcvrl = p.step("V", lambda e, i=hh % 2:
                                           e.tensor_copy(xf[i][64:128, :],
                                                         PB[6][64:128, :]),
                                           waits=[("T", ctrl)])
                            lastVat[hh % 2] = p.step(
                                "V", lambda e, pat=pat, i=hh % 2, m=m:
                                e.tensor_tensor(aT_sl[64:128, 512 * m:512 * (m + 1)],
                                                pat[64:128, :], xf[i][64:128, :],
                                                op=ALU.mult))
                    p.barrier()
                    if debug and dqa is not None and s == 0 and qt == 0:
                        p.dma("DQ", lambda e: e.dma_start(out=dqa[0], in_=Q_tl[:, :]))
                        p.dma("DQ", lambda e: e.dma_start(out=dqa[1], in_=aT_sl[:, :]))
                        p.dma("DQ", lambda e: e.dma_start(out=dmr[4:5, :], in_=RQ))
                        p.dma("DQ", lambda e: e.dma_start(out=dmr[5:6, :], in_=MS))
                        p.dma("DQ", lambda e: e.dma_start(out=dmr[6:7, 0:1],
                                                          in_=negm[0:1, 0:1]))
                        p.barrier()
                    lastTo = [0, 0]
                    uVo = [0, 0]
                    for dc in range(DC):
                        i = dc % 2
                        po = PB[i]
                        ldw = p.dma("GQ", lambda e, dc=dc, i=i:
                                    e.dma_start(out=wqo_b[i][:].rearrange(
                                        "p (c q) -> p c q", q=128),
                                        in_=wout_t[:, :, 128 * dc:
                                                   128 * (dc + 1)].rearrange(
                                            "c p q -> p c q")),
                                    waits=[("T", lastTo[i])], stream="WA")

                        def o_mms(e, i=i, po=po):
                            last = None
                            for ac in range(16):
                                last = e.matmul(po[:],
                                                wqo_b[i][:, 128 * ac:128 * (ac + 1)],
                                                aT_sl[:, 512 * ac:512 * (ac + 1)],
                                                start=(ac == 0), stop=(ac == 15))
                            return last

                        lastTo[i] = p.step("T", o_mms,
                                           waits=[("WA", ldw), ("V", uVo[i])])
                        p.step("V", lambda e, i=i, dc=dc, po=po:
                               e.tensor_single_scalar(sg[i][:, :], po[:],
                                                      bout[:, dc:dc + 1],
                                                      op=ALU.add),
                               waits=[("T", lastTo[i])])
                        uVo[i] = p.step("V", lambda e, i=i, dc=dc:
                                        e.tensor_tensor(
                                            yo[i][:, :], sg[i][:, :],
                                            y1_sl[:, 512 * dc:512 * (dc + 1)],
                                            op=ALU.add))
                        p.dma("DQ", lambda e, dc=dc, i=i, q0=q0, y2_s=y2_s:
                              e.dma_start(out=y2_s[128 * dc:128 * (dc + 1),
                                                   q0:q0 + 512],
                                          in_=yo[i][:, :]),
                              waits=[("V", uVo[i])], stream="YS")
                p.barrier()

            if "p3" in phases:
                for pi in range(NPASS):
                    ffn_pass(y2_s, oT_s, pi * TP)
                p.barrier()

        finals = [(sems[k], 16 * p.cnt[k]) for k in p.cnt
                  if k not in ("T", "V", "S", "G") and p.cnt[k] > 0]
        p.steps["Y"].append(lambda e, fs=finals: [e.wait_ge(h, v) for h, v in fs])

        with nc.Block() as blk:
            @blk.tensor
            def _(t):
                for fn in p.steps["T"]:
                    fn(t)

            @blk.vector
            def _(v):
                for fn in p.steps["V"]:
                    fn(v)

            @blk.scalar
            def _(sc):
                for fn in p.steps["S"]:
                    fn(sc)

            @blk.gpsimd
            def _(g):
                for fn in p.steps["G"]:
                    fn(g)

            @blk.sync
            def _(sy):
                for fn in p.steps["Y"]:
                    fn(sy)

    return nc



def _host_inputs(L0_attn_x, L0_attn_W_q, L0_attn_b_q, L0_attn_W_kv, L0_attn_b_kv,
                 L0_attn_W_out, L0_attn_b_out, L0_ffn_w_gate_up, L0_ffn_w_proj):
    f32 = np.float32
    perm = np.array(HEAD_PERM, dtype=np.int64)

    wgu = np.ascontiguousarray(L0_ffn_w_gate_up, dtype=f32)
    wp = np.ascontiguousarray(L0_ffn_w_proj, dtype=f32)
    wgu_h = np.ascontiguousarray(
        wgu.reshape(2, 64, 128, 16, 128).transpose(1, 4, 0, 3, 2)).reshape(64, 128, 4096)
    wp_h = np.ascontiguousarray(
        wp.reshape(16, 128, 64, 128).transpose(0, 3, 2, 1)).reshape(16, 128, 8192)

    Wq_p = np.ascontiguousarray(
        np.asarray(L0_attn_W_q).reshape(32, 64, D)[perm].reshape(D, D))
    wq_t = np.ascontiguousarray(Wq_p.T).reshape(16, 128, 2048)
    bq_p = np.ascontiguousarray(np.asarray(L0_attn_b_q).reshape(32, 64)[perm]).reshape(D)
    bq_t = np.ascontiguousarray(bq_p.reshape(16, 128).T)

    wkv_t = np.ascontiguousarray(np.asarray(L0_attn_W_kv).T).reshape(16, 128, 1024)
    bkv_t = np.ascontiguousarray(np.asarray(L0_attn_b_kv)[:512].reshape(4, 128).T)
    bv_r = np.ascontiguousarray(np.asarray(L0_attn_b_kv)[512:].reshape(1, 512))

    Wout_p = np.ascontiguousarray(
        np.asarray(L0_attn_W_out).T.reshape(32, 64, D)[perm].reshape(D, D))
    wout_t = Wout_p.reshape(16, 128, 2048)
    bout_t = np.ascontiguousarray(np.asarray(L0_attn_b_out).reshape(16, 128).T)

    pos = np.arange(S, dtype=f32)
    freq = (f32(1.0) / np.float32(10000.0) ** (np.arange(0, 64, 2, dtype=f32) / f32(64)))
    theta = pos[None, :] * freq[:, None]
    cos32 = np.cos(theta).astype(f32)
    sin32 = np.sin(theta).astype(f32)
    cos_t = np.zeros((128, S), f32)
    sin_t = np.zeros((128, S), f32)
    for hs in range(2):
        for i in range(32):
            cos_t[64 * hs + 2 * i] = cos32[i]
            cos_t[64 * hs + 2 * i + 1] = cos32[i]
            sin_t[64 * hs + 2 * i] = -sin32[i]
            sin_t[64 * hs + 2 * i + 1] = sin32[i]

    mask2 = np.zeros((128, 2), bf16)
    mask2[0:64, 0] = 1
    mask2[64:128, 1] = 1
    mask2t = np.ascontiguousarray(mask2.T)
    onesb = np.ones((128, 1), bf16)
    ones1 = np.ones((1, 128), f32)
    ones1n = -np.ones((1, 128), f32)
    onesf = np.ones((128, 64), f32)
    onesb1 = np.ones((1, 128), bf16)
    onesbn = -np.ones((1, 128), bf16)
    onesfb = np.ones((128, 64), bf16)
    r0c = np.zeros((128, 64), bf16); r0c[0, :] = 1

    shared = dict(wgu_h=wgu_h, wp_h=wp_h, wq_t=wq_t, wkv_t=wkv_t, wout_t=wout_t,
                  bq_t=bq_t, bkv_t=bkv_t, bv_r=bv_r, bout_t=bout_t,
                  cos_h=cos_t, sin_h=sin_t, mask2_h=mask2, mask2t_h=mask2t,
                  onesb_h=onesb, ones1_h=ones1, ones1n_h=ones1n, onesf_h=onesf,
                  onesb1_h=onesb1, onesbn_h=onesbn, onesfb_h=onesfb,
                  r0c_h=r0c)
    in_maps = []
    for c in range(NCORES):
        xc = np.ascontiguousarray(
            np.asarray(L0_attn_x)[NSEQ * c:NSEQ * (c + 1)].transpose(0, 2, 1),
            dtype=f32)
        m = dict(shared)
        m["xT"] = xc
        in_maps.append(m)
    return in_maps


def kernel(**inputs):
    in_maps = _host_inputs(**inputs)
    nc = build_nc(phases=PHASES, debug=DEBUG_SCRATCH)
    res = run_bass_kernel_spmd(nc, in_maps, core_ids=list(range(NCORES)))
    out = np.empty((16, S, D), np.float32)
    for c in range(NCORES):
        o = res.results[c]["oT"]
        for sq in range(NSEQ):
            out[NSEQ * c + sq] = o[sq].T
    return out


if __name__ == "__main__":
    import time
    t0 = time.time()
    nc = build_nc()
    print("instructions:", nc.n_instructions(), "build_s:", round(time.time() - t0, 1))


# version 05
# speedup vs baseline: 1.0758x; 1.0144x over previous
import contextlib
import numpy as np
import ml_dtypes
import concourse.bass as bass
import concourse.mybir as mybir
from concourse.bass_utils import run_bass_kernel_spmd

F32 = mybir.dt.float32
BF16 = mybir.dt.bfloat16
AF = mybir.ActivationFunctionType
ALU = mybir.AluOpType
AXX = mybir.AxisListType.X
bf16 = ml_dtypes.bfloat16

D = 2048
S = 2048
NSEQ = 2
NCORES = 8
TP = 512
NPASS = S // TP
DC = 16
NPAIR = 64
HC = 64
LN_EPS = 1e-5
RMS_EPS = float(np.finfo(np.float32).eps)
QTHR = float(np.float32(0.5 + 2.0 ** -9))

HEAD_A = [0, 1, 2, 3, 8, 9, 10, 11, 16, 17, 18, 19, 24, 25, 26, 27]
HEAD_PERM = []
for _m in range(16):
    HEAD_PERM += [HEAD_A[_m], HEAD_A[_m] + 4]

ENGINES = ("T", "V", "S", "G", "Y")
PHASES = ("quant", "p1", "kv", "attn", "p3")
DEBUG_SCRATCH = False
BISECT_PROJ_BARRIER = False
BISECT_GU_BARRIER = False
BISECT_H_DEDICATED = False
BISECT_LN_BARRIER = True
BISECT_LNA_BARRIER = True


class Prog:

    def __init__(self, sems):
        self.sems = sems
        self.steps = {e: [] for e in ENGINES}
        self.cnt = {k: 0 for k in sems}
        self.lastw = {}

    ESEMS = ("T", "V", "S", "G")

    def _mkwaits(self, eng, waits):
        closures = []
        for sem, val in waits:
            if val is None or val <= 0:
                continue
            v = val if sem in self.ESEMS else val * 16
            k = (eng, sem)
            if self.lastw.get(k, 0) >= v:
                continue
            self.lastw[k] = v
            h = self.sems[sem]
            closures.append(lambda e, h=h, v=v: e.wait_ge(h, v))
        return closures

    def step(self, eng, fn, waits=(), inc=True, drain=None):
        if drain is None:
            drain = eng in ("V", "G")
        closures = self._mkwaits(eng, waits)
        cnt = None
        h = None
        if inc:
            self.cnt[eng] += 1
            cnt = self.cnt[eng]
            h = self.sems[eng]

        def run(e, fn=fn, closures=closures, h=h, drain=drain):
            for c in closures:
                c(e)
            inst = fn(e)
            if h is not None:
                inst.then_inc(h, 1)
            if drain:
                e.drain()

        self.steps[eng].append(run)
        return cnt

    def dma(self, q, fn, waits=(), stream=None):
        eng = "Y" if q == "DQ" else "G"
        if stream is None:
            stream = q
        closures = self._mkwaits(eng, waits)
        self.cnt[stream] += 1
        h = self.sems[stream]

        def run(e, fn=fn, closures=closures, h=h):
            for c in closures:
                c(e)
            fn(e).then_inc(h, 16)

        self.steps[eng].append(run)
        return self.cnt[stream]

    def barrier(self):
        snap = dict(self.cnt)
        for eng in ENGINES:
            waits = [(s, snap[s]) for s in snap if s != eng]
            closures = self._mkwaits(eng, waits)
            if closures:
                self.steps[eng].append(lambda e, cs=closures: [c(e) for c in cs])


def build_nc(phases=PHASES, debug=False):
    nc = bass.Bass("TRN2", target_bir_lowering=False)

    xT = nc.dram_tensor("xT", [NSEQ, D, S], F32, kind="ExternalInput").ap()
    wgu_h = nc.dram_tensor("wgu_h", [NPAIR, 128, 4096], F32, kind="ExternalInput").ap()
    wp_h = nc.dram_tensor("wp_h", [DC, 128, 8192], F32, kind="ExternalInput").ap()
    wq_t = nc.dram_tensor("wq_t", [DC, 128, 2048], F32, kind="ExternalInput").ap()
    wkv_t = nc.dram_tensor("wkv_t", [DC, 128, 1024], F32, kind="ExternalInput").ap()
    wout_t = nc.dram_tensor("wout_t", [DC, 128, 2048], F32, kind="ExternalInput").ap()
    bq_t = nc.dram_tensor("bq_t", [128, 16], F32, kind="ExternalInput").ap()
    bkv_t = nc.dram_tensor("bkv_t", [128, 4], F32, kind="ExternalInput").ap()
    bv_r = nc.dram_tensor("bv_r", [1, 512], F32, kind="ExternalInput").ap()
    bout_t = nc.dram_tensor("bout_t", [128, 16], F32, kind="ExternalInput").ap()
    cos_h = nc.dram_tensor("cos_h", [128, S], F32, kind="ExternalInput").ap()
    sin_h = nc.dram_tensor("sin_h", [128, S], F32, kind="ExternalInput").ap()
    mask2_h = nc.dram_tensor("mask2_h", [128, 2], BF16, kind="ExternalInput").ap()
    mask2t_h = nc.dram_tensor("mask2t_h", [2, 128], BF16, kind="ExternalInput").ap()
    onesb_h = nc.dram_tensor("onesb_h", [128, 1], BF16, kind="ExternalInput").ap()
    ones1_h = nc.dram_tensor("ones1_h", [1, 128], F32, kind="ExternalInput").ap()
    ones1n_h = nc.dram_tensor("ones1n_h", [1, 128], F32, kind="ExternalInput").ap()
    onesf_h = nc.dram_tensor("onesf_h", [128, 64], F32, kind="ExternalInput").ap()
    onesb1_h = nc.dram_tensor("onesb1_h", [1, 128], BF16, kind="ExternalInput").ap()
    onesbn_h = nc.dram_tensor("onesbn_h", [1, 128], BF16, kind="ExternalInput").ap()
    onesfb_h = nc.dram_tensor("onesfb_h", [128, 64], BF16, kind="ExternalInput").ap()
    r0c_h = nc.dram_tensor("r0c_h", [128, 64], BF16, kind="ExternalInput").ap()

    skind = "ExternalOutput" if debug else "Internal"
    wgu_q = nc.dram_tensor("wgu_q", [NPAIR, 128, 4096], BF16, kind=skind).ap()
    wp_q = nc.dram_tensor("wp_q", [DC, 128, 8192], BF16, kind=skind).ap()
    y1s = nc.dram_tensor("y1s", [NSEQ, D, S], F32, kind=skind).ap()
    y2s = nc.dram_tensor("y2s", [NSEQ, D, S], F32, kind=skind).ap()
    oT = nc.dram_tensor("oT", [NSEQ, D, S], F32, kind="ExternalOutput").ap()
    if debug:
        dxn = nc.dram_tensor("dxn", [16, 128, 8192], BF16, kind="ExternalOutput").ap()
        dh = nc.dram_tensor("dh", [16, 128, 32768], BF16, kind="ExternalOutput").ap()
        drw = nc.dram_tensor("drw", [16, 16, 512], F32, kind="ExternalOutput").ap()
        dps = nc.dram_tensor("dps", [16, 128, 512], F32, kind="ExternalOutput").ap()
        dwp = nc.dram_tensor("dwp", [16, 128, 8192], BF16, kind="ExternalOutput").ap()
        dkv = nc.dram_tensor("dkv", [128, 16512], BF16, kind="ExternalOutput").ap()
        dqa = nc.dram_tensor("dqa", [2, 128, 8192], BF16, kind="ExternalOutput").ap()
        dmr = nc.dram_tensor("dmr", [8, 512], F32, kind="ExternalOutput").ap()
    else:
        dxn = dh = drw = dps = dwp = dkv = dqa = dmr = None

    with contextlib.ExitStack() as es:
        def sb(name, shape, dt):
            return es.enter_context(nc.sbuf_tensor(name, shape, dt)).ap()

        if BISECT_H_DEDICATED:
            AR1 = sb("AR1", [128, 256], F32)
            h_ded = sb("h_ded", [128, 32768], BF16)
        else:
            AR1 = sb("AR1", [128, 16384], F32)
            h_ded = None
        AR2 = sb("AR2", [128, 12288], F32)
        AR3 = sb("AR3", [128, 8192], F32)
        cos_sb = sb("cos_sb", [128, S], F32)
        sin_sb = sb("sin_sb", [128, S], F32)
        xn_sl = sb("xn_sl", [128, 8192], BF16)
        xf = [sb(f"xf{i}", [128, 512], F32) for i in range(2)]
        xb = [sb(f"xb{i}", [128, 512], BF16) for i in range(2)]
        xh = [sb(f"xh{i}", [128, 512], BF16) for i in range(2)]
        sg = [sb(f"sg{i}", [128, 512], F32) for i in range(2)]
        yo = [sb(f"yo{i}", [128, 512], F32) for i in range(2)]
        t2s = sb("t2s", [128, 512], F32)
        rbc = sb("rbc", [128, 512], F32)
        bvbc = sb("bvbc", [128, 512], F32)
        mask2 = sb("mask2", [128, 2], BF16)
        mask2t = sb("mask2t", [2, 128], BF16)
        onesb = sb("onesb", [128, 1], BF16)
        ones1 = sb("ones1", [1, 128], F32)
        ones1n = sb("ones1n", [1, 128], F32)
        onesf = sb("onesf", [128, 64], F32)
        onesb1 = sb("onesb1", [1, 128], BF16)
        onesbn = sb("onesbn", [1, 128], BF16)
        onesfb = sb("onesfb", [128, 64], BF16)
        r0c = sb("r0c", [128, 64], BF16)
        bq8 = sb("bq8", [128, 16], F32)
        bqr = sb("bqr", [128, 16], F32)
        bkv = sb("bkv", [128, 4], F32)
        bout = sb("bout", [128, 16], F32)
        negm = sb("negm", [128, 1], F32)
        ceps = sb("ceps", [128, 2], F32)

        h_sl = h_ded if BISECT_H_DEDICATED else AR1.bitcast(BF16)
        wp_b = [AR2[:, 0:4096].bitcast(BF16), AR2[:, 4096:8192].bitcast(BF16)]
        wg_b = [AR2[:, 8192:10240].bitcast(BF16), AR2[:, 10240:12288].bitcast(BF16)]

        if BISECT_H_DEDICATED:
            ARX = sb("ARX", [128, 32], F32)
            class _Z:
                def __getitem__(self, k): return ARX[0:1, 0:1]
                def bitcast(self, dt): return ARX.bitcast(dt)
                def rearrange(self, *a, **k): return ARX
            ARX_view = _Z()
        else:
            ARX = AR1
        K_sb = ARX[:, 0:4096].bitcast(BF16) if not BISECT_H_DEDICATED else ARX.bitcast(BF16)
        if BISECT_H_DEDICATED:
            V_sb = K_sb; V_v = None; E_rg = [K_sb] * 3
            krow = lambda k: ARX[0:1, 0:1]
            RK = KM = RQ = MS = RL64 = RL32 = r2_sb = mr2_sb = ARX[0:1, 0:1]
            RLH64 = RLL64 = RLH32 = RLL32 = ARX[0:1, 0:1].bitcast(BF16)
            RLBh = RLBl = ARX[0:1, 0:1].bitcast(BF16)
            RL0 = ARX[0:1, 0:1]
        else:
            V_sb = ARX[:, 4096:8256].bitcast(BF16)
            V_v = V_sb.rearrange("p (tc h v) -> p tc h v", tc=16, h=8, v=65)
            E_rg = [ARX[:, 8256 + 256 * i: 8512 + 256 * i].bitcast(BF16) for i in range(3)]
            krow = lambda k: ARX[0:1, 9024 + 512 * k: 9536 + 512 * k]
            RK, KM, RQ, MS = (krow(k) for k in range(4))
            RL64 = ARX[64:65, 9024:9536]
            RL32 = ARX[32:33, 9024:9536]
            RLH64 = ARX[64:65, 9536:10048].bitcast(BF16)[:, 0:512]
            RLL64 = ARX[64:65, 10048:10560].bitcast(BF16)[:, 0:512]
            RLH32 = ARX[32:33, 9536:10048].bitcast(BF16)[:, 0:512]
            RLL32 = ARX[32:33, 10048:10560].bitcast(BF16)[:, 0:512]
            RLBh = AR2[:, 10240:10496].bitcast(BF16)
            RLBl = AR2[:, 10496:10752].bitcast(BF16)
            RL0 = AR2[0:1, 10752:11264]
            r2_sb = ARX[0:1, 12288:14336]
            mr2_sb = ARX[0:1, 14336:16384]
        wkv_b = AR2[:, 0:8192].bitcast(BF16)
        y1_sl = AR2[:, 0:8192]
        wqo_b = [AR2[:, 8192 + 1024 * i: 9216 + 1024 * i].bitcast(BF16) for i in range(2)]
        Q_tl = AR3[:, 0:4096].bitcast(BF16)
        aT_sl = AR3[:, 4096:8192].bitcast(BF16)
        frow = lambda k: AR3[0:1, 512 * k: 512 * (k + 1)]

        q_in = [AR2[:, 0:2048].bitcast(BF16), AR2[:, 2048:4096].bitcast(BF16)]
        q_out = [AR2[:, 4096:6144].bitcast(BF16), AR2[:, 6144:8192].bitcast(BF16)]
        q_th = AR2[:, 8192:12288]
        q_aw = AR3[:, 0:2048].bitcast(BF16)
        q_qv = AR3[:, 2048:4096].bitcast(BF16)
        q_c = AR3[:, 4096:6144].bitcast(BF16)
        q_sbc = AR3[:, 6144:8192].bitcast(BF16)
        q_sc = ([ARX[0:2, 256 * i: 256 * (i + 1)].bitcast(BF16) for i in range(2)]
                if not BISECT_H_DEDICATED else [xb[0][0:2, :], xb[1][0:2, :]])

        PB = [es.enter_context(nc.psum_tensor(f"PB{i}", [128, 512], F32)).ap()
              for i in range(8)]

        sems = {k: es.enter_context(nc.semaphore(k)) for k in
                ("T", "V", "S", "G", "DQ", "GQ",
                 "WG", "WP", "XL", "YS", "WA", "KVL", "QS", "QO")}
        p = Prog(sems)

        def bcast2(bank, row_f32, hi, lo, wV, neg=False, lhs_hi=None, lhs_lo=None,
                   tile_position=None):
            ch = p.step("V", lambda e: e.tensor_copy(hi, row_f32), waits=[("V", wV)])
            cl = p.step("V", lambda e: e.tensor_tensor(lo, row_f32, hi,
                                                       op=ALU.subtract))
            l1 = lhs_hi if lhs_hi is not None else (onesbn if neg else onesb1)[:, :]
            l2 = lhs_lo if lhs_lo is not None else l1
            def mms(e, bank=bank, hi=hi, lo=lo, l1=l1, l2=l2, tp=tile_position):
                kw = {} if tp is None else dict(tile_position=tp)
                e.matmul(bank, l1, hi, start=True, stop=False, **kw)
                return e.matmul(bank, l2, lo, start=False, stop=True, **kw)
            return p.step("T", mms, waits=[("V", cl)])


        for dst, src in [(cos_sb[:], cos_h[:, :]), (sin_sb[:], sin_h[:, :]),
                         (mask2[:], mask2_h[:, :]), (mask2t[:], mask2t_h[:, :]),
                         (onesb[:], onesb_h[:, :]), (ones1[:], ones1_h[:, :]),
                         (ones1n[:], ones1n_h[:, :]), (onesf[:], onesf_h[:, :]),
                         (onesb1[:], onesb1_h[:, :]), (onesbn[:], onesbn_h[:, :]),
                         (onesfb[:], onesfb_h[:, :]), (r0c[:], r0c_h[:, :]),
                         (bqr[:], bq_t[:, :]),
                         (bkv[:], bkv_t[:, :]), (bout[:], bout_t[:, :])]:
            p.dma("DQ", lambda e, dst=dst, src=src: e.dma_start(out=dst, in_=src))
        ld0 = p.cnt["DQ"]
        p.step("S", lambda e: e.activation(bq8[:], bqr[:], AF.Copy, scale=0.125),
               waits=[("DQ", ld0)])
        ldbv = p.dma("DQ", lambda e: e.dma_start(out=yo[0][0:1, :], in_=bv_r[:, :]))
        cwv = p.step("V", lambda e: e.tensor_copy(yo[0][0:1, 0:1], yo[0][0:1, 0:1]),
                     waits=[("DQ", ldbv)])
        cT = bcast2(PB[0][:], yo[0][0:1, :],
                    yo[1][0:1, :].bitcast(BF16)[:, 0:512],
                    t2s[0:1, :].bitcast(BF16)[:, 0:512], cwv)
        p.step("V", lambda e: e.tensor_copy(bvbc[:], PB[0][:]), waits=[("T", cT)])
        p.step("V", lambda e: e.memset(ceps[:, 0:1], LN_EPS))
        p.step("V", lambda e: e.memset(ceps[:, 1:2], RMS_EPS))
        p.barrier()

        qblocks = [(wgu_h[g], wgu_q[g]) for g in range(NPAIR)]
        for d in range(DC):
            qblocks.append((wp_h[d, :, 0:4096], wp_q[d, :, 0:4096]))
            qblocks.append((wp_h[d, :, 4096:8192], wp_q[d, :, 4096:8192]))

        if "quant" in phases:
            lastVo = [0, 0]
            lastGd = [0, 0]
            lastS = [0, 0]
            lastst = [0, 0]
            pV1 = [0, 0]
            pS2 = [0, 0]
            pVa, pGb, pVw, pGd, pVo, pSaw = 0, 0, 0, 0, 0, 0
            for b, (srcb, dstb) in enumerate(qblocks):
                bi = b % 2
                ld = p.dma("GQ", lambda e, srcb=srcb, bi=bi:
                           e.dma_start(out=q_in[bi][:, :], in_=srcb),
                           waits=[("V", lastVo[bi]), ("G", lastGd[bi]),
                                  ("S", lastS[bi])], stream="QS")
                cS_aw = p.step("S", lambda e, bi=bi:
                               e.activation(q_aw[:, :], q_in[bi][:, :], AF.Abs),
                               waits=[("QS", ld), ("V", pVo), ("G", pGd)])
                lastS[bi] = cS_aw
                cT1 = 0
                cS1 = cS2 = 0
                for j in range(8):
                    qi = j % 2
                    sl = slice(512 * j, 512 * (j + 1))
                    cT1 = p.step("T", lambda e, sl=sl, qi=qi:
                                 e.matmul(PB[qi][0:2, :], mask2[:], q_aw[:, sl],
                                          start=True, stop=True),
                                 waits=[("S", cS_aw), ("V", pV1[qi])])
                    cV1 = p.step("V", lambda e, qi=qi:
                                 e.tensor_scalar(q_sc[qi][:, :], PB[qi][0:2, :],
                                                 1.0 / 64, 1e-8,
                                                 op0=ALU.mult, op1=ALU.max),
                                 waits=[("T", cT1)])
                    pV1[qi] = cV1
                    cT2 = p.step("T", lambda e, qi=qi:
                                 e.matmul(PB[2 + qi][:], mask2t[:], q_sc[qi][:, :],
                                          start=True, stop=True),
                                 waits=[("V", cV1), ("S", pS2[qi])])
                    cS1 = p.step("S", lambda e, sl=sl, qi=qi:
                                 e.activation(q_th[:, sl], PB[2 + qi][:], AF.Copy,
                                              scale=QTHR),
                                 waits=[("T", cT2), ("V", pGb)])
                    cS2 = p.step("S", lambda e, sl=sl, qi=qi:
                                 e.activation(q_sbc[:, sl], PB[2 + qi][:], AF.Copy),
                                 waits=[("V", pVw)])
                    pS2[qi] = cS2
                cVa = p.step("V", lambda e, bi=bi:
                             e.tensor_tensor(q_c[:, :], q_in[bi][:, :], q_th[:, :],
                                             op=ALU.is_gt),
                             waits=[("S", cS2), ("G", pGd)])
                cVb = p.step("V", lambda e, bi=bi:
                             e.scalar_tensor_tensor(q_aw[:, :], q_in[bi][:, :], -1.0,
                                                    q_th[:, :], op0=ALU.mult,
                                                    op1=ALU.is_gt),
                             waits=[("T", cT1)])
                cGq = p.step("G", lambda e:
                             e.tensor_tensor(q_qv[:, :], q_c[:, :], q_aw[:, :],
                                             op=ALU.subtract),
                             waits=[("V", cVb)])
                cVw = p.step("V", lambda e:
                             e.tensor_tensor(q_c[:, :], q_qv[:, :], q_sbc[:, :],
                                             op=ALU.mult),
                             waits=[("G", cGq)])
                cGd = p.step("G", lambda e, bi=bi:
                             e.tensor_tensor(q_aw[:, :], q_c[:, :], q_in[bi][:, :],
                                             op=ALU.subtract),
                             waits=[("V", cVw)])
                cVo = p.step("V", lambda e, bi=bi:
                             e.tensor_tensor(q_out[bi][:, :], q_in[bi][:, :],
                                             q_aw[:, :], op=ALU.add),
                             waits=[("G", cGd), ("QO", lastst[bi])])
                lastVo[bi] = cVo
                lastGd[bi] = cGd
                pVa, pGb, pVw, pGd, pVo, pSaw = cVa, cVb, cVw, cGd, cVo, cS_aw
                lastst[bi] = p.dma("DQ", lambda e, dstb=dstb, bi=bi:
                                   e.dma_start(out=dstb, in_=q_out[bi][:, :]),
                                   waits=[("V", cVo)], stream="QO")
            p.barrier()

        def stats_chain(rows, sum_ps, sq_ps, n, eps, eps_ap, r_out, mr_out, wT):
            m, e2, t, v, vep, s0, r0 = (rows(k) for k in range(7))
            p.step("V", lambda e: e.tensor_single_scalar(m, sum_ps, 1.0 / n,
                                                         op=ALU.mult), waits=[("T", wT)])
            p.step("V", lambda e: e.tensor_single_scalar(e2, sq_ps, 1.0 / n,
                                                         op=ALU.mult))
            p.step("V", lambda e: e.tensor_tensor(t, m, m, op=ALU.mult))
            p.step("V", lambda e: e.tensor_tensor(v, e2, t, op=ALU.subtract))
            ce = p.step("V", lambda e: e.tensor_single_scalar(vep, v, eps, op=ALU.add))
            cs = p.step("S", lambda e: e.activation(s0, v, AF.Sqrt, bias=eps_ap),
                        waits=[("V", ce)])
            p.step("V", lambda e: e.reciprocal(r0, s0), waits=[("S", cs)])
            p.step("V", lambda e: e.tensor_tensor(t, r0, r0, op=ALU.mult))
            p.step("V", lambda e: e.tensor_tensor(t, t, vep, op=ALU.mult))
            p.step("V", lambda e: e.tensor_scalar(t, t, -0.5, 1.5, op0=ALU.mult,
                                                  op1=ALU.add))
            cr = p.step("V", lambda e: e.tensor_tensor(r_out, r0, t, op=ALU.mult))
            if mr_out is not None:
                cr = p.step("V", lambda e: e.tensor_tensor(mr_out, m, r_out,
                                                           op=ALU.mult))
            return cr

        def rms_chain(rows, sq_ps, n, wT):
            t, v, vep, s0, r0, r1 = (rows(k) for k in range(6))
            cv = p.step("V", lambda e: e.tensor_single_scalar(v, sq_ps, 1.0 / n,
                                                              op=ALU.mult),
                        waits=[("T", wT)])
            p.step("V", lambda e: e.tensor_single_scalar(vep, v, RMS_EPS, op=ALU.add))
            cs = p.step("S", lambda e: e.activation(s0, v, AF.Sqrt,
                                                    bias=ceps[0:1, 1:2]),
                        waits=[("V", cv)])
            p.step("V", lambda e: e.reciprocal(r0, s0), waits=[("S", cs)])
            p.step("V", lambda e: e.tensor_tensor(t, r0, r0, op=ALU.mult))
            p.step("V", lambda e: e.tensor_tensor(t, t, vep, op=ALU.mult))
            p.step("V", lambda e: e.tensor_scalar(t, t, -0.5, 1.5, op0=ALU.mult,
                                                  op1=ALU.add))
            cr = p.step("V", lambda e: e.tensor_tensor(r1, r0, t, op=ALU.mult))
            return cr, r1

        def ln_stats(src2d, t0):
            uV = [0, 0]
            uS = [0, 0]
            uT = [0, 0]
            cT = None
            for dc in range(DC):
                i = dc % 2
                ld = p.dma("DQ", lambda e, dc=dc, i=i, src2d=src2d, t0=t0:
                           e.dma_start(out=xf[i][:, :],
                                       in_=src2d[128 * dc:128 * (dc + 1), t0:t0 + 512]),
                           waits=[("V", uV[i]), ("S", uS[i])], stream="XL")
                uV[i] = p.step("V", lambda e, i=i:
                               e.tensor_copy(xb[i][:, :], xf[i][:, :]),
                               waits=[("XL", ld), ("T", uT[i])])
                uS[i] = p.step("S", lambda e, i=i:
                               e.activation(xh[i][:, :], xf[i][:, :], AF.Square),
                               waits=[("XL", ld), ("T", uT[i])])
                p.step("T", lambda e, i=i, dc=dc:
                       e.matmul(PB[6][0:1, :], onesb[:], xb[i][:, :],
                                start=(dc == 0), stop=(dc == DC - 1)),
                       waits=[("V", uV[i])], inc=False)
                cT = p.step("T", lambda e, i=i, dc=dc:
                            e.matmul(PB[7][0:1, :], onesb[:], xh[i][:, :],
                                     start=(dc == 0), stop=(dc == DC - 1)),
                            waits=[("S", uS[i])])
                uT[i] = cT
                if BISECT_LN_BARRIER:
                    p.barrier()
            return cT, uV, uS

        def ln_apply(src2d, t0, cr, r_row, mr_row, uV0, uS0):
            bcast2(PB[6][:], r_row, frow(10).bitcast(BF16)[:, 0:512],
                   frow(11).bitcast(BF16)[:, 0:512], cr)
            cb2 = bcast2(PB[7][:], mr_row, frow(12).bitcast(BF16)[:, 0:512],
                         frow(13).bitcast(BF16)[:, 0:512], None, neg=True)
            uV = list(uV0)
            uS = list(uS0)
            lastV = None
            for dc in range(DC):
                i = dc % 2
                ld = p.dma("DQ", lambda e, dc=dc, i=i, src2d=src2d, t0=t0:
                           e.dma_start(out=xf[i][:, :],
                                       in_=src2d[128 * dc:128 * (dc + 1), t0:t0 + 512]),
                           waits=[("V", uV[i]), ("S", uS[i])], stream="XL")
                uS[i] = 0
                p.step("V", lambda e, i=i: e.tensor_tensor(sg[i][:, :], xf[i][:, :],
                                                           PB[6][:], op=ALU.mult),
                       waits=[("XL", ld), ("T", cb2)])
                lastV = p.step("V", lambda e, i=i, dc=dc:
                               e.tensor_tensor(xn_sl[:, 512 * dc:512 * (dc + 1)],
                                               sg[i][:, :], PB[7][:], op=ALU.add))
                uV[i] = lastV
                if BISECT_LN_BARRIER or BISECT_LNA_BARRIER:
                    p.barrier()
            return lastV

        passno = [0]

        def ffn_pass(src2d, dst2d, t0):
            pn = passno[0]
            passno[0] += 1
            p.barrier()
            cT_st, uV0, uS0 = ln_stats(src2d, t0)
            cr = stats_chain(frow, PB[6][0:1, :], PB[7][0:1, :], D, LN_EPS,
                             ceps[0:1, 0:1], frow(8), frow(9), cT_st)
            xn_done = ln_apply(src2d, t0, cr, frow(8), frow(9), uV0, uS0)
            if debug and dxn is not None and pn < 16:
                p.dma("DQ", lambda e, pn=pn: e.dma_start(out=dxn[pn], in_=xn_sl[:, :]),
                      waits=[("V", xn_done)])
                for k in range(10):
                    p.dma("DQ", lambda e, pn=pn, k=k:
                          e.dma_start(out=drw[pn, k:k + 1, :], in_=frow(k)),
                          waits=[("V", cr)])
            p.barrier()
            uS = [0, 0]
            uVh = [0, 0]
            uTn = [0, 0]
            lastTgu = [0, 0]
            ss_last = None
            h_done = None
            ss_pend = None

            def emit_ss(gp, ip, csqp):
                return p.step("T", lambda e, ip=ip, gp=gp:
                              e.matmul(PB[6][0:1, :], onesb[:], xh[ip][:, :],
                                       start=(gp == 0), stop=(gp == NPAIR - 1)),
                              waits=[("S", csqp)])

            for g in range(NPAIR):
                i = g % 2
                pg, pu = PB[2 + 2 * i], PB[3 + 2 * i]
                ld = p.dma("DQ", lambda e, g=g, i=i:
                           e.dma_start(out=wg_b[i][:, :], in_=wgu_q[g]),
                           waits=[("T", lastTgu[i])], stream="WG")

                for sub in range(2):
                    def gu_mms(e, i=i, pg=pg, pu=pu, sub=sub):
                        last = None
                        tgt, off = (pg, 0) if sub == 0 else (pu, 2048)
                        for dc in range(DC):
                            last = e.matmul(tgt[:],
                                            wg_b[i][:, off + 128 * dc:off + 128 * (dc + 1)],
                                            xn_sl[:, 512 * dc:512 * (dc + 1)],
                                            start=(dc == 0), stop=(dc == DC - 1))
                        return last

                    lastTgu[i] = p.step("T", gu_mms,
                                        waits=[("WG", ld), ("V", xn_done),
                                               ("S", uS[i]), ("V", uVh[i])])
                if ss_pend is not None:
                    gp, ip, csqp = ss_pend
                    ss_last = uTn[ip] = emit_ss(gp, ip, csqp)
                uS[i] = p.step("S", lambda e, i=i, pg=pg:
                               e.activation(sg[i][:, :], pg[:], AF.Silu),
                               waits=[("T", lastTgu[i]), ("V", uVh[i])])
                h_done = uVh[i] = p.step("V", lambda e, i=i, g=g, pu=pu:
                                         e.tensor_tensor(h_sl[:, 512 * g:512 * (g + 1)],
                                                         sg[i][:, :], pu[:],
                                                         op=ALU.mult),
                                         waits=[("S", uS[i])])
                cSq = p.step("S", lambda e, i=i, g=g:
                             e.activation(xh[i][:, :],
                                          h_sl[:, 512 * g:512 * (g + 1)], AF.Square),
                             waits=[("V", uVh[i]), ("T", uTn[i])])
                ss_pend = (g, i, cSq)
                if BISECT_GU_BARRIER:
                    p.barrier()
            gp, ip, csqp = ss_pend
            ss_last = uTn[ip] = emit_ss(gp, ip, csqp)
            if debug and dh is not None and pn < 16:
                p.dma("DQ", lambda e, pn=pn: e.dma_start(out=dh[pn], in_=h_sl[:, :]),
                      waits=[("V", h_done)])
            crh, rh = rms_chain(frow, PB[6][0:1, :], 8192, ss_last)
            cbr = bcast2(PB[7][:], rh, frow(10).bitcast(BF16)[:, 0:512],
                         frow(11).bitcast(BF16)[:, 0:512], crh)
            cvr = p.step("V", lambda e: e.tensor_copy(rbc[:], PB[7][:]),
                         waits=[("T", cbr)])
            if debug and drw is not None and pn < 16:
                for k in range(10, 16):
                    p.dma("DQ", lambda e, pn=pn, k=k:
                          e.dma_start(out=drw[pn, k:k + 1, :], in_=frow(k - 10)),
                          waits=[("V", crh)])
            lastTp = [0, 0]
            uVu = [0, 0]
            uVy = [0, 0]
            stc = [0, 0]
            for dcp in range(DC):
                i = dcp % 2
                pp = PB[i]
                ldw = p.dma("DQ", lambda e, dcp=dcp, i=i:
                            e.dma_start(out=wp_b[i][:, :], in_=wp_q[dcp]),
                            waits=[("T", lastTp[i])], stream="WP")
                ldx = p.dma("DQ", lambda e, dcp=dcp, i=i, src2d=src2d, t0=t0:
                            e.dma_start(out=xf[i][:, :],
                                        in_=src2d[128 * dcp:128 * (dcp + 1), t0:t0 + 512]),
                            waits=[("V", uVy[i])], stream="XL")

                for sub in range(4):
                    def proj_mms(e, i=i, pp=pp, sub=sub):
                        last = None
                        for hc in range(16 * sub, 16 * (sub + 1)):
                            last = e.matmul(pp[:], wp_b[i][:, 128 * hc:128 * (hc + 1)],
                                            h_sl[:, 512 * hc:512 * (hc + 1)],
                                            start=(hc == 0), stop=(hc == HC - 1))
                        return last

                    lastTp[i] = p.step("T", proj_mms,
                                       waits=[("WP", ldw), ("V", uVu[i]),
                                              ("V", h_done)])
                uVu[i] = p.step("V", lambda e, i=i, pp=pp:
                                e.tensor_tensor(sg[i][:, :], pp[:], rbc[:],
                                                op=ALU.mult),
                                waits=[("T", lastTp[i]), ("V", cvr)])
                uVy[i] = p.step("V", lambda e, i=i:
                                e.scalar_tensor_tensor(yo[i][:, :], sg[i][:, :], 0.5,
                                                       xf[i][:, :], op0=ALU.mult,
                                                       op1=ALU.add),
                                waits=[("XL", ldx), ("YS", stc[i])])
                stc[i] = p.dma("DQ", lambda e, dcp=dcp, i=i, dst2d=dst2d, t0=t0:
                               e.dma_start(out=dst2d[128 * dcp:128 * (dcp + 1),
                                                     t0:t0 + 512],
                                           in_=yo[i][:, :]),
                               waits=[("V", uVy[i])], stream="YS")
                if debug and dps is not None and pn == 0:
                    cpe = p.step("S", lambda e, i=i, pp=pp:
                                 e.activation(t2s[:], pp[:], AF.Copy),
                                 waits=[("V", uVu[i]), ("T", lastTp[i])])
                    p.dma("DQ", lambda e, dcp=dcp: e.dma_start(out=dps[dcp],
                                                               in_=t2s[:, :]),
                          waits=[("S", cpe)])
                    p.dma("DQ", lambda e, dcp=dcp, i=i:
                          e.dma_start(out=dwp[dcp], in_=wp_b[i][:, :]),
                          waits=[("T", lastTp[i])])
                    p.barrier()
                if BISECT_PROJ_BARRIER:
                    p.barrier()

        def rope_chunk(src_f32, dst_bf, t0):
            c1 = p.step("V", lambda e: e.stream_shuffle(t2s[:], src_f32,
                                                        [i ^ 1 for i in range(32)]))
            cg = p.step("G", lambda e, t0=t0:
                        e.tensor_tensor(t2s[:], t2s[:], sin_sb[:, t0:t0 + 512],
                                        op=ALU.mult),
                        waits=[("V", c1)])
            p.step("V", lambda e, t0=t0:
                   e.tensor_tensor(sg[1][:, :], src_f32, cos_sb[:, t0:t0 + 512],
                                   op=ALU.mult))
            cv = p.step("V", lambda e, dst_bf=dst_bf:
                        e.tensor_tensor(dst_bf, sg[1][:, :], t2s[:], op=ALU.add),
                        waits=[("G", cg)])
            return cv

        for s in range(NSEQ):
            xT_s, y1_s, y2_s, oT_s = xT[s], y1s[s], y2s[s], oT[s]

            if "p1" in phases:
                for pi in range(NPASS):
                    ffn_pass(xT_s, y1_s, pi * TP)
                p.barrier()

            if "kv" in phases:
                p.barrier()
                p.step("V", lambda e: e.memset(V_v[:, :, :, 64:65], 1.0))
                p.step("V", lambda e: e.memset(RK, 0.0))
                ldkv = p.dma("GQ", lambda e: e.dma_start(
                    out=wkv_b[:].rearrange("p (c q) -> p c q", q=1024),
                    in_=wkv_t[:, :, :].rearrange("c p q -> p c q")),
                    stream="KVL")
                for blk in range(NPASS):
                    t0 = blk * TP
                    p.barrier()
                    cT_st, uV0, uS0 = ln_stats(y1_s, t0)
                    cr2 = stats_chain(frow, PB[6][0:1, :], PB[7][0:1, :], D, LN_EPS,
                                      ceps[0:1, 0:1], r2_sb[0:1, t0:t0 + 512],
                                      mr2_sb[0:1, t0:t0 + 512], cT_st)
                    xn2_done = ln_apply(y1_s, t0, cr2, r2_sb[0:1, t0:t0 + 512],
                                        mr2_sb[0:1, t0:t0 + 512], uV0, uS0)
                    p.barrier()
                    pnn = 0
                    pVmax = [0]
                    pkT = [(0, 0), (0, 0)]
                    for kc in range(4):
                        i = kc % 2
                        pk = PB[2 + i]

                        def k_mms(e, kc=kc, pk=pk):
                            last = None
                            for dc in range(DC):
                                last = e.matmul(pk[:],
                                                wkv_b[:, 1024 * dc + 128 * kc:
                                                      1024 * dc + 128 * (kc + 1)],
                                                xn_sl[:, 512 * dc:512 * (dc + 1)],
                                                start=(dc == 0), stop=(dc == DC - 1))
                            return last

                        cTk = p.step("T", k_mms,
                                     waits=[("KVL", ldkv), ("V", xn2_done),
                                            ("S", pkT[i][0]), ("V", pkT[i][1])])
                        cVk = p.step("V", lambda e, kc=kc, pk=pk:
                                     e.tensor_single_scalar(sg[0][:, :], pk[:],
                                                            bkv[:, kc:kc + 1],
                                                            op=ALU.add),
                                     waits=[("T", cTk)])
                        cSq = p.step("S", lambda e, kc=kc, pk=pk:
                                     e.activation(xh[0][:, :], pk[:], AF.Square,
                                                  bias=bkv[:, kc:kc + 1]),
                                     waits=[("T", pnn), ("V", cVk)])
                        pkT[i] = (cSq, cVk)
                        pnn = p.step("T", lambda e: e.matmul(PB[6][0:1, :], onesb[:],
                                                             xh[0][:, :],
                                                             start=True, stop=True),
                                     waits=[("S", cSq), ("V", pVmax[0])])
                        pVmax[0] = p.step("V", lambda e, pnn=pnn:
                                          e.tensor_tensor(RK, RK, PB[6][0:1, :],
                                                          op=ALU.max),
                                          waits=[("T", pnn)])
                        rope_chunk(sg[0][:, :],
                                   K_sb[:, 2048 * kc + t0: 2048 * kc + t0 + 512], t0)
                    pvV = [0, 0]
                    for tb in range(4):
                        i = tb % 2
                        pv = PB[4 + i]

                        def v_mms(e, tb=tb, pv=pv):
                            last = None
                            for dc in range(DC):
                                last = e.matmul(pv[:],
                                                xn_sl[:, 512 * dc + 128 * tb:
                                                      512 * dc + 128 * (tb + 1)],
                                                wkv_b[:, 1024 * dc + 512:
                                                      1024 * (dc + 1)],
                                                start=(dc == 0), stop=(dc == DC - 1))
                            return last

                        cTv = p.step("T", v_mms, waits=[("V", pvV[i])])
                        tc = blk * 4 + tb
                        pvV[i] = p.step("V", lambda e, tc=tc, pv=pv:
                                        e.tensor_tensor(
                                            V_v[:, tc, :, 0:64],
                                            pv[:].rearrange("p (h v) -> p h v", v=64),
                                            bvbc[:].rearrange("p (h v) -> p h v", v=64),
                                            op=ALU.add),
                                        waits=[("T", cTv)])
                p.step("V", lambda e: e.reduce_max(KM[:, 0:1], RK, axis=AXX))
                p.barrier()
                if debug and dkv is not None and s == 0:
                    p.dma("DQ", lambda e: e.dma_start(out=dkv[:, 0:8192], in_=K_sb[:, :]))
                    p.dma("DQ", lambda e: e.dma_start(out=dkv[:, 8192:16512],
                                                      in_=V_sb[:, :]))
                    p.dma("DQ", lambda e: e.dma_start(out=dmr[0:1, :], in_=RK))
                    p.dma("DQ", lambda e: e.dma_start(out=dmr[1:2, :], in_=KM))
                    p.dma("DQ", lambda e: e.dma_start(out=dmr[2:3, :],
                                                      in_=r2_sb[0:1, 0:512]))
                    p.dma("DQ", lambda e: e.dma_start(out=dmr[3:4, :],
                                                      in_=mr2_sb[0:1, 0:512]))
                    p.barrier()

            if "attn" in phases:
                p.step("V", lambda e: e.memset(RLBh[:, :], 0.0))
                p.step("V", lambda e: e.memset(RLBl[:, :], 0.0))
                for qt in range(NPASS):
                    q0 = qt * TP
                    p.barrier()
                    y1lds = []
                    for dc in range(DC):
                        y1lds.append(p.dma("DQ", lambda e, dc=dc, y1_s=y1_s, q0=q0:
                                           e.dma_start(
                                               out=y1_sl[:, 512 * dc:512 * (dc + 1)],
                                               in_=y1_s[128 * dc:128 * (dc + 1),
                                                        q0:q0 + 512]),
                                           stream="XL"))
                    cbA = bcast2(PB[6][:], r2_sb[0:1, q0:q0 + 512],
                                 krow(4).bitcast(BF16)[:, 0:512],
                                 krow(5).bitcast(BF16)[:, 0:512], None)
                    p.step("V", lambda e: e.tensor_copy(MS[:, 8:9], MS[:, 8:9]),
                           waits=[("T", cbA)])
                    cb2 = bcast2(PB[7][:], mr2_sb[0:1, q0:q0 + 512],
                                 krow(4).bitcast(BF16)[:, 0:512],
                                 krow(5).bitcast(BF16)[:, 0:512], None, neg=True)
                    xn2_done = None
                    for dc in range(DC):
                        p.step("V", lambda e, dc=dc:
                               e.tensor_tensor(sg[dc % 2][:, :],
                                               y1_sl[:, 512 * dc:512 * (dc + 1)],
                                               PB[6][:], op=ALU.mult),
                               waits=[("XL", y1lds[dc]), ("T", cb2)])
                        xn2_done = p.step("V", lambda e, dc=dc:
                                          e.tensor_tensor(
                                              xn_sl[:, 512 * dc:512 * (dc + 1)],
                                              sg[dc % 2][:, :], PB[7][:], op=ALU.add))
                    p.step("V", lambda e: e.memset(RQ, 0.0))
                    p.barrier()
                    lastTq = [0, 0]
                    uSq2 = [0, 0]
                    uVq = [0, 0]
                    pnn = 0
                    pVmax = [0]
                    nn_pend = None

                    def emit_nn(csqp):
                        nonlocal pnn
                        pnn = p.step("T", lambda e: e.matmul(PB[6][0:1, :],
                                                             onesb[:],
                                                             xh[0][:, :],
                                                             start=True, stop=True),
                                     waits=[("S", csqp), ("V", pVmax[0])])
                        pVmax[0] = p.step("V", lambda e, pnn=pnn:
                                          e.tensor_tensor(RQ, RQ, PB[6][0:1, :],
                                                          op=ALU.max),
                                          waits=[("T", pnn)])

                    for qc in range(16):
                        i = qc % 2
                        pq = PB[i]
                        ldw = p.dma("GQ", lambda e, qc=qc, i=i:
                                    e.dma_start(out=wqo_b[i][:].rearrange(
                                        "p (c q) -> p c q", q=128),
                                        in_=wq_t[:, :, 128 * qc:128 * (qc + 1)].rearrange(
                                            "c p q -> p c q")),
                                    waits=[("T", lastTq[i])], stream="WA")

                        def q_mms(e, i=i, pq=pq):
                            last = None
                            for dc in range(DC):
                                last = e.matmul(pq[:],
                                                wqo_b[i][:, 128 * dc:128 * (dc + 1)],
                                                xn_sl[:, 512 * dc:512 * (dc + 1)],
                                                start=(dc == 0), stop=(dc == DC - 1))
                            return last

                        lastTq[i] = p.step("T", q_mms,
                                           waits=[("WA", ldw), ("V", xn2_done),
                                                  ("S", uSq2[i]), ("V", uVq[i])])
                        if nn_pend is not None:
                            emit_nn(nn_pend)
                        p.step("V", lambda e, qc=qc, pq=pq:
                               e.tensor_single_scalar(sg[0][:, :], pq[:],
                                                      bqr[:, qc:qc + 1],
                                                      op=ALU.add),
                               waits=[("T", lastTq[i])])
                        uVq[i] = p.step("V", lambda e:
                                        e.tensor_single_scalar(yo[0][:, :],
                                                               sg[0][:, :], 0.125,
                                                               op=ALU.mult))
                        uSq2[i] = p.step("S", lambda e, qc=qc, pq=pq:
                                         e.activation(xh[0][:, :], pq[:], AF.Square,
                                                      bias=bq8[:, qc:qc + 1],
                                                      scale=0.125),
                                         waits=[("T", pnn), ("V", uVq[i])])
                        nn_pend = uSq2[i]
                        rope_chunk(yo[0][:, :], Q_tl[:, 512 * qc:512 * (qc + 1)], q0)
                    emit_nn(nn_pend)
                    p.step("V", lambda e: e.reduce_max(MS[:, 0:1], RQ, axis=AXX))
                    cvm = p.step("V", lambda e: e.tensor_tensor(MS[:, 1:2], MS[:, 0:1],
                                                                KM[:, 0:1], op=ALU.mult))
                    csm = p.step("S", lambda e: e.activation(MS[:, 2:3], MS[:, 1:2],
                                                             AF.Sqrt),
                                 waits=[("V", cvm)])
                    cvn = p.step("V", lambda e: e.tensor_single_scalar(
                        MS[:, 3:4], MS[:, 2:3], -1.0, op=ALU.mult), waits=[("S", csm)])
                    cvn = p.step("V", lambda e: e.tensor_copy(
                        MS.bitcast(BF16)[:, 8:9], MS[:, 3:4]))
                    ctm = p.step("T", lambda e: e.matmul(PB[6][:, 0:1], onesb1[:, :],
                                                         MS.bitcast(BF16)[:, 8:9],
                                                         start=True, stop=True),
                                 waits=[("V", cvn)])
                    cvb = p.step("V", lambda e: e.tensor_copy(negm[:, 0:1],
                                                              PB[6][:, 0:1]),
                                 waits=[("T", ctm)])
                    p.barrier()
                    uExpBank = [0, 0, 0, 0]
                    pcrl1 = [0]
                    uPV = {0: 0, 1: 0, 2: 0}
                    lastVat = [0, 0]
                    pcvrl = 0
                    for hh in range(32):
                        m, slot = hh // 2, hh % 2
                        head = HEAD_PERM[hh]
                        g = head // 4
                        kc = head // 8
                        off = 64 * slot
                        pat = PB[5 + 2 * (hh % 2)]
                        for kb in range(16):
                            pss = (PB[3], PB[4], PB[0], PB[1])[kb % 4]
                            cqk = p.step("T",
                                         lambda e, kb=kb, kc=kc, off=off, m=m, pss=pss:
                                         e.matmul(pss[:],
                                                  K_sb[off:off + 64,
                                                       2048 * kc + 128 * kb:
                                                       2048 * kc + 128 * (kb + 1)],
                                                  Q_tl[off:off + 64,
                                                       512 * m:512 * (m + 1)],
                                                  start=True, stop=True,
                                                  tile_position=(off, 0)),
                                         waits=[("V", cvb), ("S", uExpBank[kb % 4])])
                            cexp = p.step("S", lambda e, kb=kb, pss=pss:
                                          e.activation(E_rg[kb % 3][:, :], pss[:],
                                                       AF.Exp, bias=negm[:, 0:1]),
                                          waits=[("T", cqk), ("T", uPV[kb % 3])])
                            uExpBank[kb % 4] = cexp
                            if slot == 0:
                                uPV[kb % 3] = p.step(
                                    "T", lambda e, kb=kb, g=g, pat=pat:
                                    e.matmul(pat[0:65, :], V_v[:, kb, g, :],
                                             E_rg[kb % 3][:, :],
                                             start=(kb == 0), stop=(kb == 15)),
                                    waits=[("S", cexp), ("V", lastVat[hh % 2])])
                            else:
                                p.step("T", lambda e, kb=kb, g=g, pat=pat:
                                       e.matmul(pat[64:128, :],
                                                V_v[:, kb, g, 0:64],
                                                E_rg[kb % 3][:, :],
                                                start=(kb == 0), stop=(kb == 15),
                                                tile_position=(0, 64)),
                                       waits=[("S", cexp), ("V", lastVat[hh % 2])],
                                       inc=False)
                                uPV[kb % 3] = p.step(
                                    "T", lambda e, kb=kb:
                                    e.matmul(PB[2][0:1, :], onesb[:],
                                             E_rg[kb % 3][:, :],
                                             start=(kb == 0), stop=(kb == 15)),
                                    waits=[("V", pcrl1[0])] if kb == 0 else ())
                        if slot == 0:
                            crl = p.step("V", lambda e, pat=pat:
                                         e.reciprocal(RL64, pat[64:65, :]),
                                         waits=[("T", uPV[15 % 3])])
                            ctrl = bcast2(PB[6][0:64, :], RL64, RLH64, RLL64, pcvrl,
                                          lhs_hi=onesfb[64:65, 0:64],
                                          tile_position=(64, 0))
                            pcvrl = p.step("V", lambda e, i=hh % 2:
                                           e.tensor_copy(xf[i][0:64, :],
                                                         PB[6][0:64, :]),
                                           waits=[("T", ctrl)])
                            lastVat[hh % 2] = p.step(
                                "V", lambda e, pat=pat, i=hh % 2, m=m:
                                e.tensor_tensor(aT_sl[0:64, 512 * m:512 * (m + 1)],
                                                pat[0:64, :], xf[i][0:64, :],
                                                op=ALU.mult))
                        else:
                            crl = p.step("V", lambda e:
                                         e.reciprocal(RL0, PB[2][0:1, :]),
                                         waits=[("T", uPV[15 % 3])])
                            pcrl1[0] = crl
                            p.step("V", lambda e: e.tensor_copy(RLBh[0:1, :], RL0),
                                   waits=[("V", pcvrl)])
                            cl2 = p.step("V", lambda e:
                                         e.tensor_tensor(RLBl[0:1, :], RL0,
                                                         RLBh[0:1, :],
                                                         op=ALU.subtract))
                            def rl1_mms(e):
                                e.matmul(PB[6][64:128, :], r0c[:], RLBh[:, :],
                                         start=True, stop=False,
                                         tile_position=(0, 64))
                                return e.matmul(PB[6][64:128, :], r0c[:], RLBl[:, :],
                                                start=False, stop=True,
                                                tile_position=(0, 64))
                            ctrl = p.step("T", rl1_mms, waits=[("V", cl2)])
                            pcvrl = p.step("V", lambda e, i=hh % 2:
                                           e.tensor_copy(xf[i][64:128, :],
                                                         PB[6][64:128, :]),
                                           waits=[("T", ctrl)])
                            lastVat[hh % 2] = p.step(
                                "V", lambda e, pat=pat, i=hh % 2, m=m:
                                e.tensor_tensor(aT_sl[64:128, 512 * m:512 * (m + 1)],
                                                pat[64:128, :], xf[i][64:128, :],
                                                op=ALU.mult))
                    p.barrier()
                    if debug and dqa is not None and s == 0 and qt == 0:
                        p.dma("DQ", lambda e: e.dma_start(out=dqa[0], in_=Q_tl[:, :]))
                        p.dma("DQ", lambda e: e.dma_start(out=dqa[1], in_=aT_sl[:, :]))
                        p.dma("DQ", lambda e: e.dma_start(out=dmr[4:5, :], in_=RQ))
                        p.dma("DQ", lambda e: e.dma_start(out=dmr[5:6, :], in_=MS))
                        p.dma("DQ", lambda e: e.dma_start(out=dmr[6:7, 0:1],
                                                          in_=negm[0:1, 0:1]))
                        p.barrier()
                    lastTo = [0, 0]
                    uVo = [0, 0]
                    for dc in range(DC):
                        i = dc % 2
                        po = PB[i]
                        ldw = p.dma("GQ", lambda e, dc=dc, i=i:
                                    e.dma_start(out=wqo_b[i][:].rearrange(
                                        "p (c q) -> p c q", q=128),
                                        in_=wout_t[:, :, 128 * dc:
                                                   128 * (dc + 1)].rearrange(
                                            "c p q -> p c q")),
                                    waits=[("T", lastTo[i])], stream="WA")

                        def o_mms(e, i=i, po=po):
                            last = None
                            for ac in range(16):
                                last = e.matmul(po[:],
                                                wqo_b[i][:, 128 * ac:128 * (ac + 1)],
                                                aT_sl[:, 512 * ac:512 * (ac + 1)],
                                                start=(ac == 0), stop=(ac == 15))
                            return last

                        lastTo[i] = p.step("T", o_mms,
                                           waits=[("WA", ldw), ("V", uVo[i])])
                        p.step("V", lambda e, i=i, dc=dc, po=po:
                               e.tensor_single_scalar(sg[i][:, :], po[:],
                                                      bout[:, dc:dc + 1],
                                                      op=ALU.add),
                               waits=[("T", lastTo[i])])
                        uVo[i] = p.step("V", lambda e, i=i, dc=dc:
                                        e.tensor_tensor(
                                            yo[i][:, :], sg[i][:, :],
                                            y1_sl[:, 512 * dc:512 * (dc + 1)],
                                            op=ALU.add))
                        p.dma("DQ", lambda e, dc=dc, i=i, q0=q0, y2_s=y2_s:
                              e.dma_start(out=y2_s[128 * dc:128 * (dc + 1),
                                                   q0:q0 + 512],
                                          in_=yo[i][:, :]),
                              waits=[("V", uVo[i])], stream="YS")
                p.barrier()

            if "p3" in phases:
                for pi in range(NPASS):
                    ffn_pass(y2_s, oT_s, pi * TP)
                p.barrier()

        finals = [(sems[k], 16 * p.cnt[k]) for k in p.cnt
                  if k not in ("T", "V", "S", "G") and p.cnt[k] > 0]
        p.steps["Y"].append(lambda e, fs=finals: [e.wait_ge(h, v) for h, v in fs])

        with nc.Block() as blk:
            @blk.tensor
            def _(t):
                for fn in p.steps["T"]:
                    fn(t)

            @blk.vector
            def _(v):
                for fn in p.steps["V"]:
                    fn(v)

            @blk.scalar
            def _(sc):
                for fn in p.steps["S"]:
                    fn(sc)

            @blk.gpsimd
            def _(g):
                for fn in p.steps["G"]:
                    fn(g)

            @blk.sync
            def _(sy):
                for fn in p.steps["Y"]:
                    fn(sy)

    return nc



def _host_inputs(L0_attn_x, L0_attn_W_q, L0_attn_b_q, L0_attn_W_kv, L0_attn_b_kv,
                 L0_attn_W_out, L0_attn_b_out, L0_ffn_w_gate_up, L0_ffn_w_proj):
    f32 = np.float32
    perm = np.array(HEAD_PERM, dtype=np.int64)

    wgu = np.ascontiguousarray(L0_ffn_w_gate_up, dtype=f32)
    wp = np.ascontiguousarray(L0_ffn_w_proj, dtype=f32)
    wgu_h = np.ascontiguousarray(
        wgu.reshape(2, 64, 128, 16, 128).transpose(1, 4, 0, 3, 2)).reshape(64, 128, 4096)
    wp_h = np.ascontiguousarray(
        wp.reshape(16, 128, 64, 128).transpose(0, 3, 2, 1)).reshape(16, 128, 8192)

    Wq_p = np.ascontiguousarray(
        np.asarray(L0_attn_W_q).reshape(32, 64, D)[perm].reshape(D, D))
    wq_t = np.ascontiguousarray(Wq_p.T).reshape(16, 128, 2048)
    bq_p = np.ascontiguousarray(np.asarray(L0_attn_b_q).reshape(32, 64)[perm]).reshape(D)
    bq_t = np.ascontiguousarray(bq_p.reshape(16, 128).T)

    wkv_t = np.ascontiguousarray(np.asarray(L0_attn_W_kv).T).reshape(16, 128, 1024)
    bkv_t = np.ascontiguousarray(np.asarray(L0_attn_b_kv)[:512].reshape(4, 128).T)
    bv_r = np.ascontiguousarray(np.asarray(L0_attn_b_kv)[512:].reshape(1, 512))

    Wout_p = np.ascontiguousarray(
        np.asarray(L0_attn_W_out).T.reshape(32, 64, D)[perm].reshape(D, D))
    wout_t = Wout_p.reshape(16, 128, 2048)
    bout_t = np.ascontiguousarray(np.asarray(L0_attn_b_out).reshape(16, 128).T)

    pos = np.arange(S, dtype=f32)
    freq = (f32(1.0) / np.float32(10000.0) ** (np.arange(0, 64, 2, dtype=f32) / f32(64)))
    theta = pos[None, :] * freq[:, None]
    cos32 = np.cos(theta).astype(f32)
    sin32 = np.sin(theta).astype(f32)
    cos_t = np.zeros((128, S), f32)
    sin_t = np.zeros((128, S), f32)
    for hs in range(2):
        for i in range(32):
            cos_t[64 * hs + 2 * i] = cos32[i]
            cos_t[64 * hs + 2 * i + 1] = cos32[i]
            sin_t[64 * hs + 2 * i] = -sin32[i]
            sin_t[64 * hs + 2 * i + 1] = sin32[i]

    mask2 = np.zeros((128, 2), bf16)
    mask2[0:64, 0] = 1
    mask2[64:128, 1] = 1
    mask2t = np.ascontiguousarray(mask2.T)
    onesb = np.ones((128, 1), bf16)
    ones1 = np.ones((1, 128), f32)
    ones1n = -np.ones((1, 128), f32)
    onesf = np.ones((128, 64), f32)
    onesb1 = np.ones((1, 128), bf16)
    onesbn = -np.ones((1, 128), bf16)
    onesfb = np.ones((128, 64), bf16)
    r0c = np.zeros((128, 64), bf16); r0c[0, :] = 1

    shared = dict(wgu_h=wgu_h, wp_h=wp_h, wq_t=wq_t, wkv_t=wkv_t, wout_t=wout_t,
                  bq_t=bq_t, bkv_t=bkv_t, bv_r=bv_r, bout_t=bout_t,
                  cos_h=cos_t, sin_h=sin_t, mask2_h=mask2, mask2t_h=mask2t,
                  onesb_h=onesb, ones1_h=ones1, ones1n_h=ones1n, onesf_h=onesf,
                  onesb1_h=onesb1, onesbn_h=onesbn, onesfb_h=onesfb,
                  r0c_h=r0c)
    in_maps = []
    for c in range(NCORES):
        xc = np.ascontiguousarray(
            np.asarray(L0_attn_x)[NSEQ * c:NSEQ * (c + 1)].transpose(0, 2, 1),
            dtype=f32)
        m = dict(shared)
        m["xT"] = xc
        in_maps.append(m)
    return in_maps


def kernel(**inputs):
    in_maps = _host_inputs(**inputs)
    nc = build_nc(phases=PHASES, debug=DEBUG_SCRATCH)
    res = run_bass_kernel_spmd(nc, in_maps, core_ids=list(range(NCORES)))
    out = np.empty((16, S, D), np.float32)
    for c in range(NCORES):
        o = res.results[c]["oT"]
        for sq in range(NSEQ):
            out[NSEQ * c + sq] = o[sq].T
    return out


if __name__ == "__main__":
    import time
    t0 = time.time()
    nc = build_nc()
    print("instructions:", nc.n_instructions(), "build_s:", round(time.time() - t0, 1))
